# version 27
# baseline (speedup 1.0000x reference)
import math
import numpy as np
from contextlib import ExitStack

import concourse.bass as bass
import concourse.mybir as mybir
from concourse.bass_utils import run_bass_kernel_spmd

F32 = mybir.dt.float32
F32R = mybir.dt.float32r
I32 = mybir.dt.int32
BF16 = mybir.dt.bfloat16
ALU = mybir.AluOpType
ACT = mybir.ActivationFunctionType
AX = mybir.AxisListType

LN_EPS = 1e-5
RMS_EPS = 1e-6
QP = 127.0
KNORM_EPS2 = 1e-24


class CFG:
    def __init__(self, D=2048, L=2048, B=2, M=64, RK=128, CH=128, MAC=256,
                 n_cores=8):
        self.D, self.L, self.B, self.M, self.RK = D, L, B, M, RK
        self.CH, self.MAC = CH, MAC
        self.DCH = D // 128
        self.NT = B * L // CH
        self.NTB = L // CH
        self.CPM = MAC // CH
        self.NMAC = L // MAC
        self.SCALE = 1.0 / math.sqrt(M)
        self.SEG = MAC + 3
        self.n_cores = n_cores


class Prog:
    ENGS = ("sync", "vector", "scalar", "tensor", "gpsimd")

    def __init__(self, nc):
        self.nc = nc
        self.streams = {e: [] for e in self.ENGS}
        self.cnt = {}
        self.sems = {}
        self.dirty = {e: set() for e in self.ENGS}
        self.dirty_r = {e: set() for e in self.ENGS}
        self.ctx = ExitStack()

    def sem(self, name):
        self.sems[name] = self.ctx.enter_context(self.nc.semaphore(name))
        self.cnt[name] = 0

    @staticmethod
    def _clash(aset, bset):
        for a in aset:
            an, _, asub = a.partition("#")
            for b in bset:
                bn, _, bsub = b.partition("#")
                if an == bn and (not asub or not bsub or asub == bsub):
                    return True
        return False

    def op(self, eng, fn, reads=(), writes=(), inc=None):
        if eng != "sync":
            r, w = set(reads), set(writes)
            if (self._clash(r, self.dirty[eng])
                    or self._clash(w, self.dirty[eng])
                    or self._clash(w, self.dirty_r[eng])):
                self.streams[eng].append(lambda e: e.drain())
                self.dirty[eng] = set()
                self.dirty_r[eng] = set()
        if inc is not None:
            sem = self.sems[inc]
            amt = 16 if inc.startswith("d_") else 1
            self.cnt[inc] += amt

            def emit(e, fn=fn, sem=sem, amt=amt):
                fn(e).then_inc(sem, amt)
            self.streams[eng].append(emit)
        else:
            self.streams[eng].append(fn)
        if eng != "sync":
            self.dirty[eng] |= set(writes)
            self.dirty_r[eng] |= set(reads)
        return self.cnt.get(inc)

    def wait(self, eng, sem_name, val):
        if val <= 0:
            return
        s = self.sems[sem_name]
        self.streams[eng].append(lambda e, s=s, v=val: e.wait_ge(s, v))

    def emit_all(self):
        with self.nc.Block() as blk:
            streams = self.streams

            @blk.sync
            def _(e):
                for fn in streams["sync"]:
                    fn(e)

            @blk.vector
            def _(e):
                for fn in streams["vector"]:
                    fn(e)

            @blk.scalar
            def _(e):
                for fn in streams["scalar"]:
                    fn(e)

            @blk.tensor
            def _(e):
                for fn in streams["tensor"]:
                    fn(e)

            @blk.gpsimd
            def _(e):
                for fn in streams["gpsimd"]:
                    fn(e)


def build(cfg: CFG):
    nc = bass.Bass("TRN2", target_bir_lowering=False,
                   dynamic_dma_scratch_size=8192)
    D, L, B, M, RK = cfg.D, cfg.L, cfg.B, cfg.M, cfg.RK
    CH, MAC, DCH, NT, NTB, CPM, NMAC, SEG = (cfg.CH, cfg.MAC, cfg.DCH, cfg.NT,
                                             cfg.NTB, cfg.CPM, cfg.NMAC,
                                             cfg.SEG)
    LNS = math.log(cfg.SCALE)
    NB = D // 512
    NG = max(1, DCH // 4)
    H = CH // 2

    x_in = nc.dram_tensor("x", [B * L, D], F32, kind="ExternalInput").ap()
    y_out = nc.dram_tensor("y", [B * L, D], F32, kind="ExternalOutput").ap()
    ffn_w = nc.dram_tensor("ffn_w", [D, D], F32, kind="ExternalInput").ap()
    gamma2d = nc.dram_tensor("gamma2d", [DCH, 128], F32, kind="ExternalInput").ap()
    conv_wfm = nc.dram_tensor("conv_wfm", [128, DCH * 4], F32, kind="ExternalInput").ap()
    conv_bfm = nc.dram_tensor("conv_bfm", [128, DCH], F32, kind="ExternalInput").ap()
    rw_fm_in = nc.dram_tensor("rw_fm", [128, DCH], F32, kind="ExternalInput").ap()
    inproj = nc.dram_tensor("inproj", [RK, D], F32, kind="ExternalInput").ap()
    chw0 = nc.dram_tensor("chw0", [RK, D], F32, kind="ExternalInput").ap()
    chw1r = nc.dram_tensor("chw1r", [128, D], F32, kind="ExternalInput").ap()
    chb1row = nc.dram_tensor("chb1row", [1, D], F32, kind="ExternalInput").ap()
    memw = nc.dram_tensor("memw", [RK, D], F32, kind="ExternalInput").ap()
    memb2 = nc.dram_tensor("memb2", [M, 2], F32, kind="ExternalInput").ap()
    ident32_in = nc.dram_tensor("ident32", [128, 128], F32, kind="ExternalInput").ap()
    mask_in = nc.dram_tensor("mask", [128, 128], F32, kind="ExternalInput").ap()

    u1_dram = nc.dram_tensor("u1_dram", [NT, DCH * 128 * 128], F32)
    u3_dram = nc.dram_tensor("u3_dram", [NT, DCH * 128 * 128], F32)
    x2_dram = nc.dram_tensor("x2_dram", [NT * 128, D], F32)
    cc1_in = nc.dram_tensor("cc1_in", [1, 128], F32)
    cc1_out = nc.dram_tensor("cc1_out", [1, 128], F32)
    cc2_in = nc.dram_tensor("cc2_in", [1, 128], F32)
    cc2_out = nc.dram_tensor("cc2_out", [1, 128], F32)
    scr_dram = nc.dram_tensor("scr_dram", [1, 16], F32)

    p = Prog(nc)
    ctx = p.ctx
    sb = lambda n, sh, dt=F32: ctx.enter_context(nc.sbuf_tensor(n, sh, dt))

    WqT = sb("WqT", [128, DCH * D], BF16)
    qbuf = sb("qbuf", [128, D])
    xbuf = sb("xbuf", [128, D])
    xqT = sb("xqT", [128, 2 * D], BF16)
    x1mac = sb("x1mac", [128, CPM * D])
    xl2c = sb("xl2c", [128, D])
    xl2T = sb("xl2T", [128, DCH * SEG])
    shb = sb("shb", [128, DCH * MAC])
    ch0sb = sb("ch0sb", [128, MAC], BF16)
    WinT = sb("WinT", [128, DCH * RK])
    WmgT = sb("WmgT", [128, DCH * RK])
    Wch0T = sb("Wch0T", [128, DCH * RK])
    W1T = sb("W1T", [RK, D], BF16)
    Ubuf = sb("Ubuf", [128, D], BF16)
    Ssb = sb("Ssb", [M, D])
    u3buf = sb("u3buf", [128, D])
    b1row = sb("b1row_sb", [1, D], BF16)
    onesbf = sb("onesbf", [1, 128], BF16)
    mvb = sb("mvb", [128, 12 * max(1, D // 512)])
    mv = sb("mv", [128, 4])
    acol = sb("acol", [128, 2])
    nmacol = sb("nmacol", [128, 2])
    tcol = sb("tcol", [128, 4])
    gam_fm = sb("gam_fm", [128, DCH])
    gamabs = sb("gamabs", [128, DCH])
    gamc1 = sb("gamc1", [128, DCH])
    gamc1b = sb("gamc1b", [128, DCH])
    cmrun = sb("cmrun", [128, DCH])
    cm3run = sb("cm3run", [128, DCH])
    tmp4 = sb("tmp4", [128, 8])
    cmcol = sb("cmcol", [128, 1])
    wsum = sb("wsum", [128, DCH])
    wscol = sb("wscol", [128, 1])
    onecol = sb("onecol", [128, 1])
    ones1r = sb("ones1r", [1, 128])
    sc1 = sb("sc1", [1, 16])
    c1col = sb("c1col", [128, 1])
    cecol = sb("cecol", [128, 1])
    c1bcol = sb("c1bcol", [128, 1])
    cebcol = sb("cebcol", [128, 1])
    ident32 = sb("ident32_sb", [128, 128])
    maskt = sb("maskt_sb", [128, 128])
    w_fm = sb("w_fm_sb", [128, DCH * 4])
    b_fm = sb("b_fm_sb", [128, DCH])
    rw_fm = sb("rw_fm_sb", [128, DCH])
    membc = sb("membc", [M, 2])
    m2a2 = sb("m2a2", [128, 2 * CPM])
    zeros64 = sb("zeros64", [M, CH])
    zhalo = sb("zhalo", [128, DCH * 3])
    selb = sb("selb", [M, MAC])
    taub = sb("taub", [M, MAC])
    gprod = sb("gprod", [M, MAC])
    rk1 = sb("rk1", [M, MAC])
    rk2 = sb("rk2", [M, MAC])
    cum = sb("cum", [M, MAC])
    rt1 = sb("rt1", [M, MAC])
    e1t = sb("e1t", [M, MAC])
    kq = sb("kq", [M, MAC])
    itg = sb("itg", [M, MAC])
    e2t = sb("e2t", [M, MAC])
    qt0 = sb("qt0", [M, CH])
    qtb = sb("qtb", [M, 3 * CH])
    kt0 = sb("kt0", [M, CH])
    khatT = sb("khatT", [128, M], BF16)
    PTsb = sb("PTsb", [128, 128], BF16)
    knrow = sb("knrow", [1, 2 * 128])
    edec = sb("edec", [M, 2])
    lnscol = sb("lnscol", [M, 1])
    nref = sb("nref", [M, 16])

    PS = ctx.enter_context(nc.psum_tensor("PS", [128, 4096], F32))

    for s in ("d_ldx", "d_ldu", "d_ldw", "d_stu", "d_stx2", "d_sty", "d_g",
              "c_v", "c_s", "c_t", "c_g", "c_cc"):
        p.sem(s)
    Wt = p.wait
    O = p.op
    C = p.cnt

    def rsqrt_chain(vcol_ap, out_ap, mul, add, in_name, out_name, par=0):
        O("vector", lambda e, par=par: e.tensor_scalar(
            tcol[:, 2 * par:2 * par + 1], vcol_ap, float(mul), float(add),
            ALU.mult, ALU.add),
          reads=(in_name,), writes=("tcol#%d" % par,), inc="c_v")
        Wt("scalar", "c_v", C["c_v"])
        O("scalar", lambda e, par=par: e.activation(
            tcol[:, 2 * par + 1:2 * par + 2], tcol[:, 2 * par:2 * par + 1],
            ACT.Ln),
          reads=("tcol#%d" % par,), writes=("tcol#%d" % par,))
        O("scalar", lambda e, par=par: e.activation(
            out_ap, tcol[:, 2 * par + 1:2 * par + 2], ACT.Exp, scale=-0.5),
          reads=("tcol#%d" % par,), writes=(out_name,), inc="c_s")
        return C["c_s"]

    bank_free = {4: ("c_v", 0), 5: ("c_v", 0)}

    def bank_cols(bk):
        return 2048 + 512 * (bk - 4)

    O("sync", lambda e: e.dma_start(out=ident32[:, :], in_=ident32_in), inc="d_ldw")
    O("sync", lambda e: e.dma_start(out=maskt[:, :], in_=mask_in), inc="d_ldw")
    O("sync", lambda e: e.dma_start(out=w_fm[:, :], in_=conv_wfm), inc="d_ldw")
    O("sync", lambda e: e.dma_start(out=b_fm[:, :], in_=conv_bfm), inc="d_ldw")
    O("sync", lambda e: e.dma_start(out=rw_fm[:, :], in_=rw_fm_in), inc="d_ldw")
    O("sync", lambda e: e.dma_start(out=membc[:, :], in_=memb2), inc="d_ldw")
    O("gpsimd", lambda e: e.dma_start(out=b1row[:, :], in_=chb1row), inc="d_g")
    O("sync", lambda e: e.dma_start(out=xbuf[0:DCH, 0:128], in_=gamma2d), inc="d_ldw")
    ld0 = C["d_ldw"]

    Wt("vector", "d_ldw", ld0)
    O("vector", lambda e: e.memset(PS[:, :], 0.0), writes=("PS",))
    O("vector", lambda e: e.memset(zeros64[:, :], 0.0), writes=("zeros64",))
    O("vector", lambda e: e.memset(onecol[:, :], 1.0), writes=("onecol",))
    O("vector", lambda e: e.memset(ones1r[:, :], 1.0), writes=("ones1r",))
    O("vector", lambda e: e.memset(onesbf[:, :], 1.0), writes=("onesbf",))
    O("vector", lambda e: e.memset(cmrun[:, :], 0.0), writes=("cmrun",))
    O("vector", lambda e: e.memset(cm3run[:, :], 0.0), writes=("cm3run",))
    O("vector", lambda e: e.memset(zhalo[:, :], 0.0), writes=("zhalo",))
    O("vector", lambda e: e.memset(lnscol[:, :], LNS), writes=("lnscol",),
      inc="c_v")
    vz = C["c_v"]

    Wt("tensor", "d_ldw", ld0)
    Wt("tensor", "c_v", vz)
    O("tensor", lambda e: e.transpose(PS[:, 2048:2048 + DCH],
                                      xbuf[0:DCH, 0:128],
                                      ident32[0:DCH, 0:DCH]), inc="c_t")
    xstage_free = C["c_t"]
    Wt("scalar", "c_t", C["c_t"])
    O("scalar", lambda e: e.copy(gam_fm[:, :], PS[:, 2048:2048 + DCH]),
      writes=("gam_fm",))
    O("scalar", lambda e: e.activation(gamabs[:, :], gam_fm[:, :], ACT.Abs),
      reads=("gam_fm",), writes=("gamabs",), inc="c_s")
    gam_done = C["c_s"]
    bank_free[4] = ("c_s", gam_done)

    def wprep(dram_ap, dst, dst_name, to_f32r):
        nonlocal xstage_free
        Wt("sync", "c_t", xstage_free)
        ldc = O("sync", lambda e: e.dma_start(out=xbuf[:, 0:D], in_=dram_ap),
                inc="d_ldw")
        Wt("tensor", "d_ldw", ldc)
        last = 0
        for c in range(DCH):
            bk = 4 + (c % 2)
            sname, sval = bank_free[bk]
            Wt("tensor", sname, sval)
            O("tensor", lambda e, c=c, bk=bk: e.transpose(
                PS[:, bank_cols(bk):bank_cols(bk) + 128],
                xbuf[:, c * 128:(c + 1) * 128], ident32[:, :]), inc="c_t")
            Wt("vector", "c_t", C["c_t"])
            O("vector", lambda e, c=c, bk=bk: e.tensor_copy(
                dst[:, c * 128:(c + 1) * 128].bitcast(F32R) if to_f32r
                else dst[:, c * 128:(c + 1) * 128],
                PS[:, bank_cols(bk):bank_cols(bk) + 128]),
              writes=(dst_name,), inc="c_v")
            last = C["c_v"]
            bank_free[bk] = ("c_v", last)
        xstage_free = C["c_t"]
        return last

    wprep(inproj, WinT, "WinT", True)
    wprep(memw, WmgT, "WmgT", True)
    wprep(chw0, Wch0T, "Wch0T", True)
    w1_done = wprep(chw1r, W1T, "W1T", False)

    wstage = x1mac[:, 0:D]
    wqstage = x1mac[:, D:2 * D]
    prev_red = 0
    for c in range(DCH):
        if c > 0:
            Wt("sync", "c_v", prev_red)
        else:
            Wt("sync", "c_v", w1_done)
        ldc = O("sync", lambda e, c=c: e.dma_start(
            out=wstage, in_=ffn_w[c * 128:(c + 1) * 128, :]), inc="d_ldw")
        Wt("vector", "d_ldw", ldc)
        O("vector", lambda e, c=c: e.tensor_reduce(
            wsum[:, c:c + 1], wstage, AX.X, ALU.add,
            apply_absolute_value=True), reads=("x1mac",), writes=("wsum",),
          inc="c_v")
        prev_red = C["c_v"]
    O("vector", lambda e: e.tensor_reduce(wscol[:, :], wsum[:, :], AX.X,
                                          ALU.add),
      reads=("wsum",), writes=("wscol",), inc="c_v")
    Wt("tensor", "c_v", C["c_v"])
    O("tensor", lambda e: e.matmul(PS[0:1, 3072:3073], onecol[:, :],
                                   wscol[:, :], start=True, stop=True),
      inc="c_t")
    Wt("vector", "c_t", C["c_t"])
    O("vector", lambda e: e.tensor_scalar(
        sc1[0:1, 0:1], PS[0:1, 3072:3073], 1.0 / (D * D), None, ALU.mult),
      writes=("sc1",))
    O("vector", lambda e: e.reciprocal(sc1[0:1, 1:2], sc1[0:1, 0:1]),
      reads=("sc1",), writes=("sc1",), inc="c_v")
    wsv = C["c_v"]
    O("gpsimd", lambda e, v=wsv: e.wait_ge(p.sems["c_v"], v))
    O("gpsimd", lambda e: e.dma_start(out=scr_dram.ap()[0:1, 0:2],
                                      in_=sc1[0:1, 0:2]), inc="d_g")
    O("gpsimd", lambda e, v=C["d_g"]: e.wait_ge(p.sems["d_g"], v))
    O("gpsimd", lambda e: e.dma_start(
        out=wscol[:, 0:1], in_=scr_dram.ap()[0:1, 1:2].broadcast_to((128, 1))),
      inc="d_g")
    g_ws = C["d_g"]

    Wt("vector", "d_g", g_ws)
    wst = [x1mac[:, 0:D], x1mac[:, D:2 * D]]
    wqs = [xl2c[:, :], u3buf[:, :]]
    wqs_nm = ["xl2c", "u3buf"]
    w_ld = [0] * DCH
    w_free = [0, 0]
    w_tr = [0, 0]
    Wt("sync", "d_ldw", C["d_ldw"])
    Wt("sync", "c_v", prev_red)
    for c in range(2):
        O("sync", lambda e, c=c, par=c % 2: e.dma_start(
            out=wst[par], in_=ffn_w[c * 128:(c + 1) * 128, :]), inc="d_ldw")
        w_ld[c] = C["d_ldw"]
        Wt("sync", "d_ldw", w_ld[c])
    for c in range(DCH):
        par = c % 2
        Wt("vector", "d_ldw", w_ld[c])
        O("vector", lambda e, par=par: e.tensor_scalar(
            qbuf[:, :], wst[par], wscol[:, 0:1], 1.0, ALU.mult, ALU.min),
          reads=("x1mac#%d" % par, "wscol"), writes=("qbuf",), inc="c_v")
        w_free[par] = C["c_v"]
        if c + 2 < DCH:
            Wt("sync", "c_v", w_free[par])
            O("sync", lambda e, c=c, par=par: e.dma_start(
                out=wst[par], in_=ffn_w[(c + 2) * 128:(c + 3) * 128, :]),
              inc="d_ldw")
            w_ld[c + 2] = C["d_ldw"]
            Wt("sync", "d_ldw", w_ld[c + 2])
        O("vector", lambda e: e.tensor_scalar(
            qbuf[:, :], qbuf[:, :], -1.0, None, ALU.max),
          reads=("qbuf",), writes=("qbuf",))
        O("vector", lambda e: e.tensor_scalar(
            qbuf[:, :], qbuf[:, :], 12582912.0, None, ALU.add),
          reads=("qbuf",), writes=("qbuf",))
        if c >= 2:
            Wt("vector", "c_t", w_tr[par])
        O("vector", lambda e, par=par: e.tensor_scalar(
            wqs[par], qbuf[:, :], -12582912.0, None, ALU.add),
          reads=("qbuf",), writes=(wqs_nm[par],), inc="c_v")
        w_vq = C["c_v"]
        Wt("tensor", "c_v", w_vq)
        for t in range(DCH):
            bk = 4 + (t % 2)
            sname, sval = bank_free[bk]
            Wt("tensor", sname, sval)
            O("tensor", lambda e, t=t, bk=bk, par=par: e.transpose(
                PS[:, bank_cols(bk):bank_cols(bk) + 128],
                wqs[par][:, t * 128:(t + 1) * 128], ident32[:, :]), inc="c_t")
            Wt("scalar", "c_t", C["c_t"])
            O("scalar", lambda e, t=t, c=c, bk=bk: e.copy(
                WqT[:, t * D + c * 128:t * D + (c + 1) * 128],
                PS[:, bank_cols(bk):bank_cols(bk) + 128]),
              writes=("WqT",), inc="c_s")
            bank_free[bk] = ("c_s", C["c_s"])
        w_tr[par] = C["c_t"]

    wprep_t = C["c_t"]
    wprep_g = C["c_v"]

    prev_vact = [0, 0]
    prev_spill = 0
    prev_red1 = 0
    prev_tr_last = 0
    xst = [xbuf[:, :], x1mac[:, 0:D]]
    xst_names = ["xbuf", "x1mac"]
    p1_ld = [0] * NT
    p1_sv = [0] * NT
    ngrp = max(1, D // 512)

    def p1_load(t):
        par = t % 2
        if t >= 2:
            Wt("sync", "c_v", prev_vact[par])
        elif t == 1:
            Wt("sync", "d_ldx", p1_ld[0])
            Wt("sync", "c_v", w_free[0])
        O("sync", lambda e, t=t, par=par: e.dma_start(
            out=xst[par], in_=x_in[t * 128:(t + 1) * 128, :]), inc="d_ldx")
        p1_ld[t] = C["d_ldx"]

    def p1_stats(t):
        par = t % 2
        nm = xst_names[par]
        Wt("vector", "d_ldx", p1_ld[t])
        for g in range(ngrp):
            O("vector", lambda e, g=g, par=par: e.bn_stats(
                mvb[:, 6 * (par * ngrp + g):6 * (par * ngrp + g + 1)],
                xst[par][:, 512 * g:512 * (g + 1)]),
              reads=(nm,), writes=("mvb#%d" % par,))
        O("vector", lambda e, par=par: e.bn_aggr(
            mv[:, 2 * par:2 * par + 2],
            mvb[:, 6 * par * ngrp:6 * (par + 1) * ngrp]),
          reads=("mvb#%d" % par,), writes=("mv#%d" % par,), inc="c_v")
        rsqrt_chain(mv[:, 2 * par + 1:2 * par + 2],
                    acol[:, par:par + 1], 1.0 + RMS_EPS, RMS_EPS * LN_EPS,
                    "mv#%d" % par, "acol#%d" % par, par=par)
        Wt("vector", "c_s", C["c_s"])
        O("vector", lambda e, par=par: e.tensor_scalar(
            nmacol[:, par:par + 1], mv[:, 2 * par:2 * par + 1],
            acol[:, par:par + 1], -1.0, ALU.mult, ALU.mult),
          reads=("mv#%d" % par, "acol#%d" % par),
          writes=("nmacol#%d" % par,), inc="c_v")
        p1_sv[t] = C["c_v"]

    p1_load(0)
    p1_stats(0)
    for t in range(NT):
        par = t % 2
        if t + 1 < NT:
            p1_load(t + 1)
            p1_stats(t + 1)
        if t > 0:
            Wt("vector", "c_t", prev_tr_last)
        O("vector", lambda e, par=par: e.tensor_scalar(
            qbuf[:, :], xst[par], acol[:, par:par + 1],
            nmacol[:, par:par + 1], ALU.mult, ALU.add),
          reads=(xst_names[par], "nmacol#%d" % par, "acol#%d" % par),
          writes=("qbuf",), inc="c_v")
        prev_vact[par] = C["c_v"]
        Wt("tensor", "c_v", prev_vact[par])
        for g in range(NG):
            bk = 4 + (g % 2)
            sname, sval = bank_free[bk]
            Wt("tensor", sname, sval)
            nch = min(4, DCH - g * 4)
            for j in range(nch):
                c = g * 4 + j
                O("tensor", lambda e, c=c, bk=bk, j=j: e.transpose(
                    PS[:, bank_cols(bk) + j * 128:bank_cols(bk) + (j + 1) * 128],
                    qbuf[:, c * 128:(c + 1) * 128], ident32[:, :]), inc="c_t")
            prev_tr_last = C["c_t"]
            Wt("scalar", "c_t", prev_tr_last)
            if t > 0 and g == 0:
                Wt("scalar", "d_stu", prev_spill)
                Wt("scalar", "c_v", prev_red1)
            O("scalar", lambda e, g=g, bk=bk, nch=nch: e.copy(
                u3buf[:, g * 512:g * 512 + nch * 128],
                PS[:, bank_cols(bk):bank_cols(bk) + nch * 128]),
              writes=("u3buf#%d" % g,), inc="c_s")
            ev = C["c_s"]
            bank_free[bk] = ("c_s", ev)
            Wt("vector", "c_s", ev)
            O("vector", lambda e, g=g, nch=nch: e.tensor_reduce(
                tmp4[:, (g % 2) * 4:(g % 2) * 4 + nch],
                u3buf[:, g * 512:g * 512 + nch * 128].rearrange(
                    "p (c f) -> p c f", f=128),
                AX.X, ALU.max, apply_absolute_value=True),
              reads=("u3buf#%d" % g,), writes=("tmp4#%d" % (g % 2),))
            O("vector", lambda e, g=g, nch=nch: e.tensor_tensor(
                cmrun[:, g * 4:g * 4 + nch], cmrun[:, g * 4:g * 4 + nch],
                tmp4[:, (g % 2) * 4:(g % 2) * 4 + nch], ALU.max),
              reads=("tmp4#%d" % (g % 2), "cmrun#%d" % g),
              writes=("cmrun#%d" % g,), inc="c_v")
            prev_red1 = C["c_v"]
            Wt("scalar", "c_s", ev)
            O("scalar", lambda e, t=t, g=g, nch=nch: e.dma_start(
                out=u1_dram.ap()[t:t + 1,
                                 g * 4 * 128 * 128:(g * 4 + nch) * 128 * 128]
                .rearrange("one (c p f) -> (one p) c f", p=128, f=128),
                in_=u3buf[:, g * 512:g * 512 + nch * 128]), inc="d_stu")
            prev_spill = C["d_stu"]
    ph1_tr_last = prev_tr_last
    ph1_spill = prev_spill
    ph1_red = prev_red1

    def allreduce_scale(cmr_name, cmr, cc_in_t, cc_out_t, sidx, c1c, cec,
                        gamdst, gamdst_name):
        O("vector", lambda e: e.tensor_tensor(cmr[:, :], cmr[:, :],
                                              gamabs[:, :], ALU.mult),
          reads=(cmr_name, "gamabs"), writes=(cmr_name,))
        O("vector", lambda e: e.tensor_reduce(cmcol[:, :], cmr[:, :], AX.X,
                                              ALU.max),
          reads=(cmr_name,), writes=("cmcol",), inc="c_v")
        v0 = C["c_v"]
        O("gpsimd", lambda e, v=v0: e.wait_ge(p.sems["c_v"], v))
        O("gpsimd", lambda e: e.dma_start(
            out=cc_in_t.ap().rearrange("one (p f) -> (one p) f", f=1),
            in_=cmcol[:, 0:1]), inc="d_g")
        O("gpsimd", lambda e, v=C["d_g"]: e.wait_ge(p.sems["d_g"], v))
        O("gpsimd", lambda e: e.collective_compute(
            "AllReduce", ALU.max, replica_groups=[list(range(cfg.n_cores))],
            ins=[cc_in_t.ap()], outs=[cc_out_t.ap()]), inc="c_cc")
        O("gpsimd", lambda e, v=C["c_cc"]: e.wait_ge(p.sems["c_cc"], v))
        O("gpsimd", lambda e, v=C["d_stu"]: e.wait_ge(p.sems["d_stu"], v))
        O("gpsimd", lambda e: e.dma_start(out=u3buf[0:1, 0:128],
                                          in_=cc_out_t.ap()), inc="d_g")
        gv = C["d_g"]
        Wt("vector", "d_g", gv)
        O("vector", lambda e, sidx=sidx: e.tensor_reduce(
            sc1[0:1, sidx:sidx + 1], u3buf[0:1, 0:128], AX.X, ALU.max),
          reads=("u3buf",), writes=("sc1",))
        O("vector", lambda e, sidx=sidx: e.reciprocal(
            sc1[0:1, sidx + 1:sidx + 2], sc1[0:1, sidx:sidx + 1]),
          reads=("sc1",), writes=("sc1",))
        O("vector", lambda e, sidx=sidx: e.tensor_scalar(
            sc1[0:1, sidx + 2:sidx + 3], sc1[0:1, sidx + 1:sidx + 2], QP,
            None, ALU.mult), reads=("sc1",), writes=("sc1",))
        O("vector", lambda e, sidx=sidx: e.tensor_scalar(
            sc1[0:1, sidx + 3:sidx + 4], sc1[0:1, sidx:sidx + 1],
            sc1[0:1, 0:1], 0.5 / QP, ALU.mult, ALU.mult),
          reads=("sc1",), writes=("sc1",), inc="c_v")
        v1 = C["c_v"]
        O("gpsimd", lambda e, v=v1: e.wait_ge(p.sems["c_v"], v))
        O("gpsimd", lambda e, sidx=sidx: e.dma_start(
            out=scr_dram.ap()[0:1, sidx + 2:sidx + 4],
            in_=sc1[0:1, sidx + 2:sidx + 4]), inc="d_g")
        O("gpsimd", lambda e, v=C["d_g"]: e.wait_ge(p.sems["d_g"], v))
        O("gpsimd", lambda e, sidx=sidx: e.dma_start(
            out=c1c[:, 0:1],
            in_=scr_dram.ap()[0:1, sidx + 2:sidx + 3].broadcast_to((128, 1))),
          inc="d_g")
        O("gpsimd", lambda e, sidx=sidx: e.dma_start(
            out=cec[:, 0:1],
            in_=scr_dram.ap()[0:1, sidx + 3:sidx + 4].broadcast_to((128, 1))),
          inc="d_g")
        gv2 = C["d_g"]
        Wt("vector", "d_g", gv2)
        O("vector", lambda e: e.tensor_scalar(
            gamdst[:, :], gam_fm[:, :], c1c[:, 0:1], None, ALU.mult),
          reads=("gam_fm",), writes=(gamdst_name,), inc="c_v")
        return C["c_v"]

    ph2_ready = allreduce_scale("cmrun", cmrun, cc1_in, cc1_out, 4,
                                c1col, cecol, gamc1, "gamc1")
    ph1_red = ph2_ready
    b1_g = C["d_g"]

    b03_free = ("c_v", 0)
    last_qb = 0
    last_xq_any = C["c_v"]
    last_ffn = [0, 0]
    last_x1 = 0
    xbuf_free = ("c_v", 0)
    last_xl2ready = 0
    last_conv = 0
    last_o_t = wprep_t
    last_proj_t = 0
    last_sh2 = 0
    last_dsdec = 0
    last_PT_ev = 0
    last_khatT_ev = 0
    last_Ut = 0
    last_u3_spill = ph1_spill
    last_u3_red = ph1_red
    last_x2_spill = 0
    last_x2b = 0
    last_v3 = 0
    last_xl2re = 0
    last_LN3_tr = ph1_tr_last
    last_gprod = 0
    last_kq = 0

    Wt("sync", "c_v", ph2_ready)

    hoist = {}
    qstate = {"last_xq_any": last_xq_any, "last_ffn": last_ffn,
              "xl2c_free_t": last_LN3_tr,
              "u3s": C["c_s"], "u3d": C["d_stu"]}

    def emit_quant_load(tg, dram, nm):
        Wt("sync", "c_v", qstate["last_xq_any"])
        ldu = [0] * 4
        for gq in range(4):
            O("sync", lambda e, tg=tg, gq=gq, dram=dram: e.dma_start(
                out=qbuf[:, gq * 512:(gq + 1) * 512],
                in_=dram.ap()[tg:tg + 1,
                              gq * 4 * 128 * 128:(gq + 1) * 4 * 128 * 128]
                .rearrange("one (c p f) -> (one p) c f", p=128, f=128)),
              inc="d_ldu")
            ldu[gq] = C["d_ldu"]
            Wt("sync", "d_ldu", ldu[gq])
        return ldu

    def emit_quant_scale(gq, ldu, gsc, gnm):
        Wt("vector", "d_ldu", ldu[gq])
        for c in range(gq * 4, gq * 4 + 4):
            O("vector", lambda e, c=c, gsc=gsc: e.tensor_scalar(
                qbuf[:, c * 128:(c + 1) * 128],
                qbuf[:, c * 128:(c + 1) * 128],
                gsc[:, c:c + 1], None, ALU.mult),
              reads=("qbuf#%d" % c, gnm), writes=("qbuf#%d" % c,))

    def emit_quant_round(tg):
        kk = tg % 2
        O("vector", lambda e: e.tensor_scalar(
            qbuf[:, :], qbuf[:, :], 12582912.0, None, ALU.add),
          reads=("qbuf",), writes=("qbuf",))
        if tg > 1:
            Wt("vector", "c_t", qstate["last_ffn"][kk])
        O("vector", lambda e, kk=kk: e.tensor_scalar(
            xqT[:, kk * D:(kk + 1) * D], qbuf[:, :], -12582912.0, None,
            ALU.add),
          reads=("qbuf",), writes=("xqT",), inc="c_v")
        qstate["last_xq_any"] = C["c_v"]

    def emit_quant(tg):
        ldu = emit_quant_load(tg, u1_dram, "gamc1")
        for gq in range(4):
            emit_quant_scale(gq, ldu, gamc1, "gamc1")
        emit_quant_round(tg)

    emit_quant(0)
    for b in range(B):
        O("vector", lambda e: e.tensor_copy(
            xl2T[:, :].bitcast(F32R).rearrange(
                "p (c s) -> p c s", s=SEG)[:, :, 0:3],
            zhalo[:, :].rearrange("p (c s) -> p c s", s=3)),
          reads=("zhalo",), writes=("xl2T",), inc="c_v")
        zh = C["c_v"]
        for mi in range(NMAC):
            segs = xl2T[:, :].rearrange("p (c s) -> p c s", s=SEG)
            accs = shb[:, :].rearrange("p (c s) -> p c s", s=MAC)
            shr = shb[:, :].rearrange("p (c s) -> p c s", s=MAC)
            shrr = shb[:, :].bitcast(F32R).rearrange("p (c s) -> p c s",
                                                     s=MAC)
            segr = xl2T[:, :].bitcast(F32R).rearrange("p (c s) -> p c s",
                                                      s=SEG)
            silg = [0] * NG
            for k in range(CPM):
                tg = b * NTB + mi * CPM + k
                kk = tg % 2
                if hoist.get("tg") == tg:
                    ldx = hoist["ldx"]
                    ffn_bk = hoist["ffn_bk"]
                    last_ffn[kk] = ffn_bk[NB - 1]
                else:
                    sn, sv = xbuf_free
                    Wt("sync", sn, sv)
                    O("sync", lambda e, tg=tg: e.dma_start(
                        out=xbuf[:, :], in_=x_in[tg * 128:(tg + 1) * 128, :]),
                      inc="d_ldx")
                    ldx = C["d_ldx"]
                    Wt("tensor", "c_v", qstate["last_xq_any"])
                    sn, sv = b03_free
                    Wt("tensor", sn, sv)
                    ffn_bk = [0] * NB
                    for nb in range(NB):
                        for c in range(DCH):
                            O("tensor", lambda e, c=c, nb=nb, kk=kk: e.matmul(
                                PS[:, nb * 512:(nb + 1) * 512],
                                xqT[:, kk * D + c * 128:
                                    kk * D + (c + 1) * 128],
                                WqT[:, c * D + nb * 512:
                                    c * D + (nb + 1) * 512],
                                start=(c == 0), stop=(c == DCH - 1)),
                              inc="c_t" if c == DCH - 1 else None)
                        ffn_bk[nb] = C["c_t"]
                    last_ffn[kk] = C["c_t"]
                do_q = tg + 1 < NT
                if do_q:
                    qldu = emit_quant_load(tg + 1, u1_dram, "gamc1")
                Wt("vector", "d_ldx", ldx)
                if mi > 0 or b > 0:
                    Wt("vector", "d_stx2", last_x2_spill)
                for nb in range(NB):
                    if do_q:
                        emit_quant_scale(nb, qldu, gamc1, "gamc1")
                    Wt("vector", "c_t", ffn_bk[nb])
                    O("vector", lambda e, k=k, nb=nb: e.scalar_tensor_tensor(
                        x1mac[:, k * D + nb * 512:k * D + (nb + 1) * 512],
                        PS[:, nb * 512:(nb + 1) * 512], cecol[:, 0:1],
                        xbuf[:, nb * 512:(nb + 1) * 512], ALU.mult, ALU.add),
                      reads=("xbuf", "cecol"), writes=("x1mac#b%d" % nb,),
                      inc="c_v" if nb == NB - 1 else None)
                    O("vector", lambda e, k=k, nb=nb: e.bn_stats(
                        mvb[:, 6 * nb:6 * (nb + 1)],
                        x1mac[:, k * D + 512 * nb:k * D + 512 * (nb + 1)]),
                      reads=("x1mac#b%d" % nb,), writes=("mvb#%d" % nb,))
                last_x1 = C["c_v"]
                b03_free = ("c_v", last_x1)
                xbuf_free = ("c_v", last_x1)
                O("vector", lambda e: e.bn_aggr(mv[:, 0:2], mvb[:, 0:24]),
                  reads=("mvb",), writes=("mv",), inc="c_v")
                rsqrt_chain(mv[:, 1:2], m2a2[:, 2 * k + 1:2 * k + 2], 1.0,
                            LN_EPS, "mv", "m2a2")
                Wt("vector", "c_s", C["c_s"])
                O("vector", lambda e, k=k: e.tensor_scalar(
                    m2a2[:, 2 * k:2 * k + 1], mv[:, 0:1],
                    m2a2[:, 2 * k + 1:2 * k + 2], -1.0, ALU.mult, ALU.mult),
                  reads=("mv", "m2a2"), writes=("m2a2",), inc="c_v")
                m2a2_v = C["c_v"]
                Wt("vector", "c_t", qstate["xl2c_free_t"])
                O("vector", lambda e, k=k: e.tensor_scalar(
                    xl2c[:, :], x1mac[:, k * D:(k + 1) * D],
                    m2a2[:, 2 * k + 1:2 * k + 2], m2a2[:, 2 * k:2 * k + 1],
                    ALU.mult, ALU.add),
                  reads=("x1mac", "m2a2"), writes=("xl2c",), inc="c_v")
                xl2_s = C["c_v"]
                if do_q:
                    emit_quant_round(tg + 1)
                Wt("tensor", "c_v", xl2_s)
                for g in range(NG):
                    bk = 4 + (g % 2)
                    sname, sval = bank_free[bk]
                    Wt("tensor", sname, sval)
                    if mi + k == 0 and g == 0:
                        Wt("tensor", "c_v", last_dsdec)
                    nch = min(4, DCH - g * 4)
                    for j in range(nch):
                        c = g * 4 + j
                        O("tensor", lambda e, c=c, bk=bk, j=j: e.transpose(
                            PS[:, bank_cols(bk) + j * 128:
                               bank_cols(bk) + (j + 1) * 128],
                            xl2c[:, c * 128:(c + 1) * 128], ident32[:, :]),
                          inc="c_t")
                    tr = C["c_t"]
                    if g == NG - 1:
                        last_LN3_tr = tr
                        qstate["xl2c_free_t"] = tr
                    Wt("scalar", "c_t", tr)
                    if mi + k == 0:
                        Wt("scalar", "c_v", zh)
                    if g == 0 and (mi > 0 or b > 0):
                        Wt("scalar", "c_v", last_conv)
                    O("scalar", lambda e, g=g, k=k, bk=bk, nch=nch: e.copy(
                        xl2T[:, :].bitcast(F32R).rearrange(
                            "p (c s) -> p c s", s=SEG)[
                            :, g * 4:g * 4 + nch, 3 + k * CH:3 + (k + 1) * CH],
                        PS[:, bank_cols(bk):bank_cols(bk) + nch * 128].rearrange(
                            "p (c f) -> p c f", f=128)),
                      writes=("xl2T",), inc="c_s")
                    ev_g = C["c_s"]
                    bank_free[bk] = ("c_s", ev_g)
                    if k == CPM - 1:
                        grp = range(g * 4, g * 4 + 4)
                        if g == 0:
                            Wt("tensor", "c_v", last_x1)
                            Wt("tensor", "c_v",
                               max(last_dsdec, last_kq, last_x2b))
                            Wt("vector", "c_t", last_o_t)
                        Wt("tensor", "c_s", ev_g)
                        for c in grp:
                            O("tensor", lambda e, c=c: e.matmul(
                                PS[0:RK, 0:MAC],
                                Wch0T[:, c * RK:(c + 1) * RK].bitcast(F32R),
                                segr[:, c, 3:3 + MAC],
                                start=(c == 0), stop=(c == DCH - 1)))
                        Wt("vector", "c_s", ev_g)
                        for c in grp:
                            O("vector", lambda e, c=c: e.tensor_scalar(
                                accs[:, c, :].bitcast(F32R),
                                segs[:, c, 0:MAC],
                                w_fm[:, 4 * c:4 * c + 1],
                                b_fm[:, c:c + 1], ALU.mult, ALU.add),
                              reads=("xl2T#%d" % c, "w_fm", "b_fm"),
                              writes=("shb#%d" % c,))
                            for i in range(1, 4):
                                O("vector", lambda e, c=c, i=i:
                                  e.scalar_tensor_tensor(
                                    accs[:, c, :].bitcast(F32R),
                                    segs[:, c, i:i + MAC],
                                    w_fm[:, 4 * c + i:4 * c + i + 1],
                                    accs[:, c, :], ALU.mult, ALU.add),
                                  reads=("xl2T#%d" % c, "w_fm",
                                         "shb#%d" % c),
                                  writes=("shb#%d" % c,))
                        O("vector", lambda e: e.engine_nop(), inc="c_v")
                        last_conv = C["c_v"]
                        Wt("scalar", "c_v", last_conv)
                        O("scalar", lambda e, g=g: e.activation(
                            shb[:, g * 4 * MAC:(g + 1) * 4 * MAC]
                            .bitcast(F32R),
                            shb[:, g * 4 * MAC:(g + 1) * 4 * MAC], ACT.Silu),
                          reads=tuple("shb#%d" % c for c in grp),
                          writes=tuple("shb#%d" % c for c in grp),
                          inc="c_s")
                        silg[g] = C["c_s"]
            xl2_all = C["c_s"]

            for g in range(NG):
                Wt("tensor", "c_s", silg[g])
                for c in range(g * 4, g * 4 + 4):
                    st = (c == 0)
                    sp = (c == DCH - 1)
                    O("tensor", lambda e, c=c, st=st, sp=sp: e.matmul(
                        PS[0:M, 512:512 + MAC],
                        WinT[:, c * RK:c * RK + M].bitcast(F32R),
                        shrr[:, c, :], start=st, stop=sp))
                    O("tensor", lambda e, c=c, st=st, sp=sp: e.matmul(
                        PS[0:M, 1024:1024 + MAC],
                        WinT[:, c * RK + M:c * RK + RK].bitcast(F32R),
                        shrr[:, c, :], start=st, stop=sp))
                    O("tensor", lambda e, c=c, st=st, sp=sp: e.matmul(
                        PS[0:M, 3072:3072 + MAC],
                        WmgT[:, c * RK:c * RK + M].bitcast(F32R),
                        shrr[:, c, :], start=st, stop=sp))
                    O("tensor", lambda e, c=c, st=st, sp=sp: e.matmul(
                        PS[0:M, 3584:3584 + MAC],
                        WmgT[:, c * RK + M:c * RK + RK].bitcast(F32R),
                        shrr[:, c, :], start=st, stop=sp),
                      inc="c_t" if sp else None)
            O("vector", lambda e: e.tensor_copy(
                segs[:, :, 0:3].bitcast(F32R), segs[:, :, MAC:MAC + 3]),
              reads=("xl2T",), writes=("xl2T",), inc="c_v")
            last_proj_t = C["c_t"]
            Wt("scalar", "c_t", last_proj_t)
            O("scalar", lambda e: e.copy(rk1[:, :], PS[0:M, 512:512 + MAC]),
              writes=("rk1",))
            O("scalar", lambda e: e.copy(
                rk2[:, :], PS[0:M, 1024:1024 + MAC]),
              writes=("rk2",))
            O("scalar", lambda e: e.activation(
                selb[:, :], PS[0:M, 3072:3072 + MAC], ACT.Exp,
                bias=membc[:, 0:1]), reads=("membc",), writes=("selb",))
            O("scalar", lambda e: e.activation(
                selb[:, :], selb[:, :], ACT.Ln, bias=onecol[0:M, 0:1]),
              reads=("selb", "onecol"), writes=("selb",))
            O("scalar", lambda e: e.activation(
                taub[:, :], PS[0:M, 3584:3584 + MAC], ACT.Sigmoid,
                bias=membc[:, 1:2]), reads=("membc",), writes=("taub",))
            O("scalar", lambda e: e.copy(ch0sb[:, :], PS[:, 0:MAC]),
              writes=("ch0sb",), inc="c_s")
            last_projev = C["c_s"]
            bank_free[4] = ("c_s", last_projev)
            bank_free[5] = ("c_s", last_projev)
            Wt("vector", "c_s", last_projev)
            O("vector", lambda e: e.tensor_tensor(
                gprod[:, :], selb[:, :], taub[:, :], ALU.mult),
              reads=("selb", "taub"), writes=("gprod",), inc="c_v")
            last_gprod = C["c_v"]
            BL = CH // 4
            qoffs = [0, CH, CH + (CH - BL), CH + (CH - BL) + (CH - 2 * BL)]

            def emit_scoreprep(k):
                for j in range(4):
                    wj = CH - j * BL
                    qo = qoffs[j]
                    O("scalar", lambda e, j=j, wj=wj, qo=qo, k=k: e.activation(
                        qtb[:, qo:qo + wj],
                        cum[:, k * CH + j * BL:(k + 1) * CH], ACT.Exp,
                        scale=-1.0,
                        bias=cum[:, k * CH + j * BL + BL // 2:
                                 k * CH + j * BL + BL // 2 + 1]),
                      reads=("cum",), writes=("qtb",))
                    O("scalar", lambda e, j=j, k=k: e.activation(
                        kt0[:, j * BL:(j + 1) * BL],
                        cum[:, k * CH + j * BL:k * CH + (j + 1) * BL],
                        ACT.Exp, bias=nref[:, 4 * k + j:4 * k + j + 1]),
                      reads=("cum", "nref"), writes=("kt0",),
                      inc="c_s" if j == 3 else None)
                sc_exp = C["c_s"]
                Wt("vector", "c_s", sc_exp)
                for j in range(4):
                    wj = CH - j * BL
                    qo = qoffs[j]
                    O("vector", lambda e, j=j, wj=wj, qo=qo, k=k:
                      e.tensor_tensor(
                        qtb[:, qo:qo + wj],
                        rk1[:, k * CH + j * BL:(k + 1) * CH],
                        qtb[:, qo:qo + wj], ALU.mult),
                      reads=("rk1", "qtb"), writes=("qtb",))
                    O("vector", lambda e, j=j, k=k: e.tensor_tensor(
                        kt0[:, j * BL:(j + 1) * BL],
                        kq[:, k * CH + j * BL:k * CH + (j + 1) * BL],
                        kt0[:, j * BL:(j + 1) * BL], ALU.mult),
                      reads=("kq", "kt0"), writes=("kt0",))
                O("vector", lambda e: e.engine_nop(), inc="c_v")
                return C["c_v"]
            for k in range(CPM):
                O("vector", lambda e, k=k: e.tensor_tensor_scan(
                    cum[:, k * CH:(k + 1) * CH],
                    gprod[:, k * CH:(k + 1) * CH],
                    zeros64[:, :], 0.0, ALU.add, ALU.add),
                  reads=("gprod", "zeros64"), writes=("cum",),
                  inc="c_v" if k == CPM - 1 else None)
            cum_v = C["c_v"]
            Wt("scalar", "c_v", cum_v)
            O("scalar", lambda e: e.activation(
                e1t[:, :], cum[:, :], ACT.Exp, scale=-1.0,
                bias=lnscol[:, 0:1]),
              reads=("cum", "lnscol"), writes=("e1t",), inc="c_s")
            e1_s = C["c_s"]
            sn, sv = b03_free
            Wt("tensor", sn, sv)
            Wt("tensor", "d_g", b1_g)
            Wt("tensor", "c_s", last_projev)
            for nb in range(NB):
                O("tensor", lambda e, nb=nb: e.matmul(
                    PS[:, nb * 512:(nb + 1) * 512], onesbf[0:1, :],
                    b1row[0:1, nb * 512:(nb + 1) * 512],
                    start=True, stop=False))
            for nb in range(NB):
                O("tensor", lambda e, nb=nb: e.matmul(
                    PS[:, nb * 512:(nb + 1) * 512],
                    ch0sb[:, 0:CH], W1T[:, nb * 512:(nb + 1) * 512],
                    start=False, stop=True),
                  inc="c_t" if nb == NB - 1 else None)
            ch1_t = C["c_t"]
            Wt("scalar", "c_t", ch1_t)
            sn, sv = xbuf_free
            Wt("scalar", sn, sv)
            O("scalar", lambda e: e.activation(xbuf[:, :], PS[:, 0:D],
                                               ACT.Sigmoid),
              writes=("xbuf",), inc="c_s")
            sig_s = C["c_s"]
            b03_free = ("c_s", sig_s)
            Wt("vector", "c_s", e1_s)
            O("vector", lambda e: e.tensor_tensor(
                rt1[:, :].bitcast(F32R), rk1[:, :], e1t[:, :], ALU.mult),
              reads=("rk1", "e1t"), writes=("rt1",), inc="c_v")
            rt1_v = C["c_v"]
            O("vector", lambda e: e.tensor_tensor(
                itg[:, :], rk2[:, :], rk2[:, :], ALU.mult),
              reads=("rk2",), writes=("itg",), inc="c_v")
            Wt("tensor", "c_v", C["c_v"])
            Wt("tensor", "c_v", last_dsdec)
            O("tensor", lambda e: e.matmul(
                PS[0:1, 2560:2560 + MAC], onecol[0:M, 0:1], itg[:, :],
                start=True, stop=True), inc="c_t")
            kn_t = C["c_t"]
            Wt("vector", "c_t", kn_t)
            O("vector", lambda e: e.tensor_scalar(
                knrow[0:1, 0:MAC], PS[0:1, 2560:2560 + MAC], KNORM_EPS2,
                None, ALU.max), writes=("knrow",), inc="c_v")
            Wt("scalar", "c_v", C["c_v"])
            O("scalar", lambda e: e.activation(
                knrow[0:1, 0:MAC], knrow[0:1, 0:MAC], ACT.Ln),
              reads=("knrow",), writes=("knrow",))
            O("scalar", lambda e: e.activation(
                knrow[0:1, 0:MAC], knrow[0:1, 0:MAC], ACT.Exp, scale=-0.5),
              reads=("knrow",), writes=("knrow",), inc="c_s")
            kn_s = C["c_s"]
            Wt("tensor", "c_s", kn_s)
            O("tensor", lambda e: e.matmul(
                PS[0:M, 2560:2560 + MAC], ones1r[0:1, 0:M],
                knrow[0:1, 0:MAC], start=True, stop=True), inc="c_t")
            knb_t = C["c_t"]
            O("scalar", lambda e: e.activation(
                itg[:, :], selb[:, :], ACT.Ln),
              reads=("selb",), writes=("itg",), inc="c_s")
            Wt("vector", "c_s", C["c_s"])
            O("vector", lambda e: e.tensor_tensor(
                itg[:, :], itg[:, :], taub[:, :], ALU.mult),
              reads=("itg", "taub"), writes=("itg",), inc="c_v")
            Wt("scalar", "c_v", C["c_v"])
            O("scalar", lambda e: e.activation(itg[:, :], itg[:, :], ACT.Exp),
              reads=("itg",), writes=("itg",), inc="c_s")
            itg_s = C["c_s"]
            Wt("vector", "c_t", knb_t)
            Wt("vector", "c_s", itg_s)
            O("vector", lambda e: e.scalar_tensor_tensor(
                kq[:, :], rk2[:, :], 1.0, PS[0:M, 2560:2560 + MAC],
                ALU.bypass, ALU.mult),
              reads=("rk2",), writes=("kq",))
            O("vector", lambda e: e.tensor_tensor(
                kq[:, :], kq[:, :], itg[:, :], ALU.mult),
              reads=("kq", "itg"), writes=("kq",), inc="c_v")
            last_kq = C["c_v"]
            O("vector", lambda e: e.tensor_scalar(
                nref[:, 0:8],
                cum[:, :].rearrange("p (a f) -> p a f", f=32)[:, :, 16:17]
                .rearrange("p a one -> p (a one)"),
                -1.0, None, ALU.mult),
              reads=("cum",), writes=("nref",))
            O("vector", lambda e: e.tensor_scalar(
                nref[:, 8:10],
                cum[:, :].rearrange("p (a f) -> p a f", f=CH)[:, :, CH - 1:CH]
                .rearrange("p a one -> p (a one)"),
                -1.0, None, ALU.mult),
              reads=("cum",), writes=("nref",), inc="c_v")
            nref_v = C["c_v"]
            Wt("scalar", "c_v", nref_v)
            for k in range(CPM):
                O("scalar", lambda e, k=k: e.activation(
                    e2t[:, k * CH:(k + 1) * CH], cum[:, k * CH:(k + 1) * CH],
                    ACT.Exp, bias=nref[:, 8 + k:9 + k]),
                  reads=("cum", "nref"), writes=("e2t",))
            O("scalar", lambda e: e.activation(
                edec[:, 0:2], nref[:, 8:10], ACT.Exp),
              reads=("nref",), writes=("edec",), inc="c_s")
            e2_s = C["c_s"]
            Wt("vector", "c_s", e2_s)
            O("vector", lambda e: e.tensor_tensor(
                e2t[:, :], kq[:, :], e2t[:, :], ALU.mult),
              reads=("kq", "e2t"), writes=("e2t",), inc="c_v")
            khat_v = C["c_v"]
            sc_mul = emit_scoreprep(0)
            Wt("vector", "c_t", last_LN3_tr)
            O("vector", lambda e: e.tensor_scalar(
                xl2c[:, :], x1mac[:, 0:D], m2a2[:, 1:2], m2a2[:, 0:1],
                ALU.mult, ALU.add),
              reads=("x1mac", "m2a2"), writes=("xl2c",))
            Wt("vector", "c_s", sig_s)
            O("vector", lambda e: e.tensor_tensor(
                Ubuf[:, :], xbuf[:, :], xl2c[:, :], ALU.mult),
              reads=("xbuf", "xl2c"), writes=("Ubuf",), inc="c_v")
            last_Ut = C["c_v"]
            xbuf_free = ("c_v", last_Ut)
            for c in range(DCH):
                O("vector", lambda e, c=c: e.tensor_scalar(
                    accs[:, c, :].bitcast(F32R), accs[:, c, :],
                    rw_fm[:, c:c + 1], None, ALU.mult),
                  reads=("shb#%d" % c, "rw_fm"), writes=("shb#%d" % c,))
            O("vector", lambda e: e.engine_nop(), inc="c_v")
            last_sh2 = C["c_v"]

            for k in range(CPM):
                tg = b * NTB + mi * CPM + k
                ksl = slice(k * CH, (k + 1) * CH)
                if k > 0:
                    sc_mul = emit_scoreprep(k)
                Wt("tensor", "c_v", sc_mul)
                if tg > 0:
                    Wt("tensor", "c_v", last_PT_ev)
                for j in range(4):
                    wj = CH - j * BL
                    qo = qoffs[j]
                    O("tensor", lambda e, j=j, wj=wj, qo=qo: e.matmul(
                        PS[j * BL:(j + 1) * BL, 2816 + j * BL:2816 + CH],
                        kt0[:, j * BL:(j + 1) * BL], qtb[:, qo:qo + wj],
                        start=True, stop=True,
                        tile_position=(0, (j * BL) // 32 * 32)),
                      inc="c_t" if j == 3 else None)
                pt_t = C["c_t"]
                Wt("vector", "c_t", pt_t)
                O("vector", lambda e: e.tensor_tensor(
                    PTsb[:, :], PS[:, 2816:2816 + CH], maskt[:, :], ALU.mult),
                  reads=("maskt",), writes=("PTsb",), inc="c_v")
                last_PT_ev = C["c_v"]
                Wt("tensor", "c_v", max(khat_v, last_PT_ev, last_kq))
                O("tensor", lambda e, ksl=ksl: e.transpose(
                    PS[0:CH, 2560:2560 + M], e2t[:, ksl], ident32[0:M, 0:M]),
                  inc="c_t")
                khT_t = C["c_t"]
                Wt("vector", "c_t", khT_t)
                O("vector", lambda e: e.tensor_copy(
                    khatT[:, :], PS[0:CH, 2560:2560 + M]),
                  writes=("khatT",), inc="c_v")
                last_khatT_ev = C["c_v"]

                sn, sv = b03_free
                Wt("tensor", sn, sv)
                Wt("tensor", "c_v", max(last_Ut, last_PT_ev, rt1_v))
                first_chunk = (mi == 0 and k == 0)
                for nb in range(NB):
                    O("tensor", lambda e, nb=nb, fc=first_chunk: e.matmul(
                        PS[:, nb * 512:(nb + 1) * 512], PTsb[:, :],
                        Ubuf[:, nb * 512:(nb + 1) * 512],
                        start=True, stop=fc),
                      inc="c_t" if (first_chunk and nb == NB - 1) else None)
                if not first_chunk:
                    for nb in range(NB):
                        O("tensor", lambda e, nb=nb, ksl=ksl: e.matmul(
                            PS[:, nb * 512:(nb + 1) * 512],
                            rt1[:, ksl].bitcast(F32R),
                            Ssb[:, nb * 512:(nb + 1) * 512].bitcast(F32R),
                            start=False, stop=True),
                          inc="c_t" if nb == NB - 1 else None)
                last_o_t = C["c_t"]
                Wt("vector", "c_t", last_o_t)
                O("vector", lambda e, k=k: e.scalar_tensor_tensor(
                    x1mac[:, k * D:(k + 1) * D], PS[:, 0:D], 1.0,
                    x1mac[:, k * D:(k + 1) * D], ALU.bypass, ALU.add),
                  reads=("x1mac",), writes=("x1mac",), inc="c_v")
                x2a_v = C["c_v"]
                b03_free = ("c_v", x2a_v)
                Wt("tensor", "c_v", max(last_khatT_ev, last_kq))
                bf4 = bank_free[4]
                bf5 = bank_free[5]
                Wt("tensor", bf4[0], bf4[1])
                Wt("tensor", bf5[0], bf5[1])
                for nb in range(NB):
                    O("tensor", lambda e, nb=nb: e.matmul(
                        PS[0:M, 2048 + nb * 512:2048 + (nb + 1) * 512],
                        khatT[:, :], Ubuf[:, nb * 512:(nb + 1) * 512],
                        start=True, stop=True),
                      inc="c_t" if nb == NB - 1 else None)
                ds_t = C["c_t"]
                if k + 1 < CPM:
                    sn, sv = b03_free
                    Wt("tensor", sn, sv)
                    Wt("tensor", "d_g", b1_g)
                    for nb in range(NB):
                        O("tensor", lambda e, nb=nb: e.matmul(
                            PS[:, nb * 512:(nb + 1) * 512], onesbf[0:1, :],
                            b1row[0:1, nb * 512:(nb + 1) * 512],
                            start=True, stop=False))
                    for nb in range(NB):
                        O("tensor", lambda e, nb=nb, k=k: e.matmul(
                            PS[:, nb * 512:(nb + 1) * 512],
                            ch0sb[:, (k + 1) * CH:(k + 2) * CH],
                            W1T[:, nb * 512:(nb + 1) * 512],
                            start=False, stop=True),
                          inc="c_t" if nb == NB - 1 else None)
                    ch1_t = C["c_t"]
                    Wt("scalar", "c_t", ch1_t)
                    sn, sv = xbuf_free
                    Wt("scalar", sn, sv)
                    O("scalar", lambda e: e.activation(
                        xbuf[:, :], PS[:, 0:D], ACT.Sigmoid),
                      writes=("xbuf",), inc="c_s")
                    sig_s = C["c_s"]
                    b03_free = ("c_s", sig_s)
                Wt("vector", "c_t", ds_t)
                if first_chunk:
                    O("vector", lambda e: e.tensor_scalar(
                        Ssb[:, :].bitcast(F32R), PS[0:M, 2048:2048 + D], 1.0,
                        None, ALU.mult),
                      writes=("Ssb",), inc="c_v")
                else:
                    O("vector", lambda e, k=k: e.scalar_tensor_tensor(
                        Ssb[:, :].bitcast(F32R), Ssb[:, :], edec[:, k:k + 1],
                        PS[0:M, 2048:2048 + D], ALU.mult, ALU.add),
                      reads=("Ssb", "edec"), writes=("Ssb",), inc="c_v")
                last_dsdec = C["c_v"]
                bank_free[4] = ("c_v", last_dsdec)
                bank_free[5] = ("c_v", last_dsdec)
                if k + 1 < CPM:
                    O("vector", lambda e, k=k: e.tensor_scalar(
                        xl2c[:, :], x1mac[:, (k + 1) * D:(k + 2) * D],
                        m2a2[:, 2 * k + 3:2 * k + 4],
                        m2a2[:, 2 * k + 2:2 * k + 3], ALU.mult, ALU.add),
                      reads=("x1mac", "m2a2"), writes=("xl2c",))
                    Wt("vector", "c_s", sig_s)
                    O("vector", lambda e: e.tensor_tensor(
                        Ubuf[:, :], xbuf[:, :], xl2c[:, :], ALU.mult),
                      reads=("xbuf", "xl2c"), writes=("Ubuf",), inc="c_v")
                    last_Ut = C["c_v"]
                    xbuf_free = ("c_v", last_Ut)

                Wt("tensor", "c_v", max(last_sh2, last_dsdec))
                for g in range(NG):
                    bk = 4 + (g % 2)
                    sname, sval = bank_free[bk]
                    Wt("tensor", sname, sval)
                    nch = min(4, DCH - g * 4)
                    for j in range(nch):
                        c = g * 4 + j
                        O("tensor", lambda e, c=c, bk=bk, j=j, k=k: e.transpose(
                            PS[:, bank_cols(bk) + j * 128:
                               bank_cols(bk) + (j + 1) * 128],
                            shr[:, c, k * CH:(k + 1) * CH], ident32[:, :]),
                          inc="c_t")
                    sh_tr = C["c_t"]
                    Wt("vector", "c_t", sh_tr)
                    O("vector", lambda e, g=g, k=k, bk=bk, nch=nch:
                      e.scalar_tensor_tensor(
                        x1mac[:, k * D + g * 512:k * D + g * 512 + nch * 128],
                        PS[:, bank_cols(bk):bank_cols(bk) + nch * 128], 1.0,
                        x1mac[:, k * D + g * 512:k * D + g * 512 + nch * 128],
                        ALU.bypass, ALU.add),
                      reads=("x1mac",), writes=("x1mac",), inc="c_v")
                    last_x2b = C["c_v"]
                    bank_free[bk] = ("c_v", last_x2b)

                if k == CPM - 1 and tg + 1 < NT:
                    tgn = tg + 1
                    kkn = tgn % 2
                    sn, sv = xbuf_free
                    Wt("sync", sn, sv)
                    O("sync", lambda e, tgn=tgn: e.dma_start(
                        out=xbuf[:, :],
                        in_=x_in[tgn * 128:(tgn + 1) * 128, :]),
                      inc="d_ldx")
                    hoist["ldx"] = C["d_ldx"]
                    Wt("tensor", "c_v", qstate["last_xq_any"])
                    sn, sv = b03_free
                    Wt("tensor", sn, sv)
                    hbk = [0] * NB
                    for nb in range(2):
                        for c in range(DCH):
                            O("tensor", lambda e, c=c, nb=nb, kkn=kkn:
                              e.matmul(
                                PS[:, nb * 512:(nb + 1) * 512],
                                xqT[:, kkn * D + c * 128:
                                    kkn * D + (c + 1) * 128],
                                WqT[:, c * D + nb * 512:
                                    c * D + (nb + 1) * 512],
                                start=(c == 0), stop=(c == DCH - 1)),
                              inc="c_t" if c == DCH - 1 else None)
                        hbk[nb] = C["c_t"]
                ngrp = max(1, D // 512)
                for g in range(ngrp):
                    O("vector", lambda e, g=g, k=k: e.bn_stats(
                        mvb[:, 6 * g:6 * (g + 1)],
                        x1mac[:, k * D + 512 * g:k * D + 512 * (g + 1)]),
                      reads=("x1mac", "mvb"), writes=("mvb",))
                O("vector", lambda e, ngrp=ngrp: e.bn_aggr(
                    mv[:, 0:2], mvb[:, 0:6 * ngrp]),
                  reads=("mvb",), writes=("mv",), inc="c_v")
                rsqrt_chain(mv[:, 1:2], acol[:, 0:1], 1.0 + RMS_EPS,
                            RMS_EPS * LN_EPS, "mv", "acol")
                Wt("vector", "c_s", C["c_s"])
                O("vector", lambda e: e.tensor_scalar(
                    nmacol[:, 0:1], mv[:, 0:1], acol[:, 0:1], -1.0, ALU.mult,
                    ALU.mult), reads=("mv", "acol"), writes=("nmacol",),
                  inc="c_v")
                O("vector", lambda e, k=k: e.tensor_scalar(
                    xl2c[:, :], x1mac[:, k * D:(k + 1) * D], acol[:, 0:1],
                    nmacol[:, 0:1], ALU.mult, ALU.add),
                  reads=("x1mac", "nmacol", "acol"), writes=("xl2c",),
                  inc="c_v")
                v3_v = C["c_v"]
                last_v3 = v3_v
                Wt("scalar", "c_v", v3_v)
                O("scalar", lambda e, tg=tg, k=k: e.dma_start(
                    out=x2_dram.ap()[tg * 128:(tg + 1) * 128, :],
                    in_=x1mac[:, k * D:(k + 1) * D]), inc="d_stx2")
                last_x2_spill = C["d_stx2"]
                Wt("tensor", "c_v", v3_v)
                for g in range(NG):
                    bk = 4 + (g % 2)
                    sname, sval = bank_free[bk]
                    Wt("tensor", sname, sval)
                    nch = min(4, DCH - g * 4)
                    for j in range(nch):
                        c = g * 4 + j
                        O("tensor", lambda e, c=c, bk=bk, j=j: e.transpose(
                            PS[:, bank_cols(bk) + j * 128:
                               bank_cols(bk) + (j + 1) * 128],
                            xl2c[:, c * 128:(c + 1) * 128], ident32[:, :]),
                          inc="c_t")
                    tr3 = C["c_t"]
                    last_LN3_tr = tr3
                    qstate["xl2c_free_t"] = tr3
                    Wt("scalar", "c_t", tr3)
                    if g == 0:
                        Wt("scalar", "d_stu", last_u3_spill)
                        Wt("scalar", "c_v", last_u3_red)
                    O("scalar", lambda e, g=g, bk=bk, nch=nch: e.copy(
                        u3buf[:, g * 512:g * 512 + nch * 128],
                        PS[:, bank_cols(bk):bank_cols(bk) + nch * 128]),
                      writes=("u3buf#%d" % g,), inc="c_s")
                    ev3 = C["c_s"]
                    bank_free[bk] = ("c_s", ev3)
                    Wt("vector", "c_s", ev3)
                    O("vector", lambda e, g=g, nch=nch: e.tensor_reduce(
                        tmp4[:, (g % 2) * 4:(g % 2) * 4 + nch],
                        u3buf[:, g * 512:g * 512 + nch * 128].rearrange(
                            "p (c f) -> p c f", f=128),
                        AX.X, ALU.max, apply_absolute_value=True),
                      reads=("u3buf#%d" % g,), writes=("tmp4#%d" % (g % 2),))
                    O("vector", lambda e, g=g, nch=nch: e.tensor_tensor(
                        cm3run[:, g * 4:g * 4 + nch],
                        cm3run[:, g * 4:g * 4 + nch],
                        tmp4[:, (g % 2) * 4:(g % 2) * 4 + nch], ALU.max),
                      reads=("tmp4#%d" % (g % 2), "cm3run#%d" % g),
                      writes=("cm3run#%d" % g,), inc="c_v")
                    last_u3_red = C["c_v"]
                    Wt("scalar", "c_s", ev3)
                    O("scalar", lambda e, tg=tg, g=g, nch=nch: e.dma_start(
                        out=u3_dram.ap()[tg:tg + 1,
                                         g * 4 * 128 * 128:
                                         (g * 4 + nch) * 128 * 128]
                        .rearrange("one (c p f) -> (one p) c f", p=128, f=128),
                        in_=u3buf[:, g * 512:g * 512 + nch * 128]),
                      inc="d_stu")
                    last_u3_spill = C["d_stu"]
                if k == CPM - 1 and tg + 1 < NT:
                    for nb in range(2, NB):
                        for c in range(DCH):
                            O("tensor", lambda e, c=c, nb=nb, kkn=kkn:
                              e.matmul(
                                PS[:, nb * 512:(nb + 1) * 512],
                                xqT[:, kkn * D + c * 128:
                                    kkn * D + (c + 1) * 128],
                                WqT[:, c * D + nb * 512:
                                    c * D + (nb + 1) * 512],
                                start=(c == 0), stop=(c == DCH - 1)),
                              inc="c_t" if c == DCH - 1 else None)
                        hbk[nb] = C["c_t"]
                    hoist["tg"] = tgn
                    hoist["ffn_bk"] = hbk
                qstate["u3s"] = C["c_s"]
                qstate["u3d"] = last_u3_spill

    ph3_ready = allreduce_scale("cm3run", cm3run, cc2_in, cc2_out, 8,
                                c1bcol, cebcol, gamc1b, "gamc1b")

    Wt("sync", "c_v", ph3_ready)
    Wt("sync", "d_stx2", C["d_stx2"])
    Wt("sync", "d_stu", C["d_stu"])
    ph3_ffn = [last_ffn[0], last_ffn[1]]
    ph3_y = 0
    ph3_ystore = 0
    q3 = {"last_xq": qstate["last_xq_any"]}

    def emit_quant3_round(t):
        kk = t % 2
        O("vector", lambda e: e.tensor_scalar(
            qbuf[:, :], qbuf[:, :], 12582912.0, None, ALU.add),
          reads=("qbuf",), writes=("qbuf",))
        if t > 1:
            Wt("vector", "c_t", ph3_ffn[kk])
        O("vector", lambda e, kk=kk: e.tensor_scalar(
            xqT[:, kk * D:(kk + 1) * D], qbuf[:, :], -12582912.0, None,
            ALU.add),
          reads=("qbuf",), writes=("xqT",), inc="c_v")
        q3["last_xq"] = C["c_v"]

    def emit_quant3(t):
        Wt("sync", "c_v", q3["last_xq"])
        ldu = [0] * 4
        for gq in range(4):
            O("sync", lambda e, t=t, gq=gq: e.dma_start(
                out=qbuf[:, gq * 512:(gq + 1) * 512],
                in_=u3_dram.ap()[t:t + 1,
                                 gq * 4 * 128 * 128:(gq + 1) * 4 * 128 * 128]
                .rearrange("one (c p f) -> (one p) c f", p=128, f=128)),
              inc="d_ldu")
            ldu[gq] = C["d_ldu"]
            Wt("sync", "d_ldu", ldu[gq])
        if t == 0:
            Wt("vector", "c_t", last_LN3_tr)
        for gq in range(4):
            emit_quant_scale(gq, ldu, gamc1b, "gamc1b")
        emit_quant3_round(t)

    emit_quant3(0)
    y_bk = [0] * NB
    ysto_bk = [0] * NB
    for t in range(NT):
        kk = t % 2
        if t > 0:
            Wt("sync", "c_v", ph3_y)
        else:
            sn, sv = xbuf_free
            Wt("sync", sn, sv)
        O("sync", lambda e, t=t: e.dma_start(
            out=xbuf[:, :], in_=x2_dram.ap()[t * 128:(t + 1) * 128, :]),
          inc="d_ldx")
        lx = C["d_ldx"]
        Wt("tensor", "c_v", q3["last_xq"])
        ffn3_bk = [0] * NB
        for nb in range(NB):
            if t == 0:
                if nb == 0:
                    sn, sv = b03_free
                    Wt("tensor", sn, sv)
            else:
                Wt("tensor", "c_v", y_bk[nb])
            for c in range(DCH):
                O("tensor", lambda e, c=c, nb=nb, kk=kk: e.matmul(
                    PS[:, nb * 512:(nb + 1) * 512],
                    xqT[:, kk * D + c * 128:kk * D + (c + 1) * 128],
                    WqT[:, c * D + nb * 512:c * D + (nb + 1) * 512],
                    start=(c == 0), stop=(c == DCH - 1)),
                  inc="c_t" if c == DCH - 1 else None)
            ffn3_bk[nb] = C["c_t"]
        ph3_ffn[kk] = C["c_t"]
        do_q = t + 1 < NT
        if do_q:
            Wt("sync", "c_v", q3["last_xq"])
            qldu = [0] * 4
            for gq in range(4):
                O("sync", lambda e, t=t, gq=gq: e.dma_start(
                    out=qbuf[:, gq * 512:(gq + 1) * 512],
                    in_=u3_dram.ap()[t + 1:t + 2,
                                     gq * 4 * 128 * 128:
                                     (gq + 1) * 4 * 128 * 128]
                    .rearrange("one (c p f) -> (one p) c f", p=128, f=128)),
                  inc="d_ldu")
                qldu[gq] = C["d_ldu"]
                Wt("sync", "d_ldu", qldu[gq])
        Wt("vector", "d_ldx", lx)
        for nb in range(NB):
            if do_q:
                emit_quant_scale(nb, qldu, gamc1b, "gamc1b")
                if nb == NB - 1:
                    O("vector", lambda e: e.tensor_scalar(
                        qbuf[:, :], qbuf[:, :], 12582912.0, None, ALU.add),
                      reads=("qbuf",), writes=("qbuf",))
            Wt("vector", "c_t", ffn3_bk[nb])
            if t > 0:
                Wt("vector", "d_sty", ysto_bk[nb])
            O("vector", lambda e, nb=nb: e.scalar_tensor_tensor(
                u3buf[:, nb * 512:(nb + 1) * 512],
                PS[:, nb * 512:(nb + 1) * 512], cebcol[:, 0:1],
                xbuf[:, nb * 512:(nb + 1) * 512], ALU.mult, ALU.add),
              reads=("xbuf", "cebcol"), writes=("u3buf#%d" % nb,),
              inc="c_v")
            y_bk[nb] = C["c_v"]
            Wt("scalar", "c_v", y_bk[nb])
            O("scalar", lambda e, t=t, nb=nb: e.dma_start(
                out=y_out[t * 128:(t + 1) * 128, nb * 512:(nb + 1) * 512],
                in_=u3buf[:, nb * 512:(nb + 1) * 512]), inc="d_sty")
            ysto_bk[nb] = C["d_sty"]
            Wt("scalar", "d_sty", ysto_bk[nb])
        ph3_y = y_bk[NB - 1]
        if do_q:
            kk2 = (t + 1) % 2
            if t + 1 > 1:
                Wt("vector", "c_t", ph3_ffn[kk2])
            O("vector", lambda e, kk2=kk2: e.tensor_scalar(
                xqT[:, kk2 * D:(kk2 + 1) * D], qbuf[:, :], -12582912.0, None,
                ALU.add),
              reads=("qbuf",), writes=("xqT",), inc="c_v")
            q3["last_xq"] = C["c_v"]

    Wt("sync", "d_sty", C["d_sty"])
    p.emit_all()
    p.ctx.close()
    return nc


def make_in_maps(cfg, inputs, n_cores):
    D, L, B = cfg.D, cfg.L, cfg.B
    x = np.ascontiguousarray(inputs["L0_attn_hidden_states"], np.float32)
    conv_w = np.asarray(inputs["L0_attn_conv_w"], np.float32)
    conv_b = np.asarray(inputs["L0_attn_conv_b"], np.float32)
    w_fm = np.ascontiguousarray(
        conv_w.reshape(cfg.DCH, 128, 4).transpose(1, 0, 2).reshape(128, -1))
    b_fm = np.ascontiguousarray(conv_b.reshape(cfg.DCH, 128).T)
    rw_fm = np.ascontiguousarray(
        np.asarray(inputs["L0_attn_residual_weight"], np.float32)
        .reshape(cfg.DCH, 128).T)
    gamma2d = np.ascontiguousarray(
        np.asarray(inputs["L0_ffn_norm_weight"], np.float32)
        .reshape(cfg.DCH, 128))
    chw1 = np.asarray(inputs["L0_attn_ch_gate_w1"], np.float32)
    chw1r = np.ascontiguousarray(
        chw1.reshape(cfg.DCH, 128, cfg.RK).transpose(1, 0, 2).reshape(128, -1))
    memb = np.asarray(inputs["L0_attn_mem_gate_b"], np.float32)
    memb2 = np.ascontiguousarray(memb.reshape(2, cfg.M).T)
    import ml_dtypes
    ident32 = np.eye(128, dtype=np.float32)
    sidx = np.arange(128)[:, None]
    tidx = np.arange(128)[None, :]
    mask = np.where(tidx >= sidx, np.float32(cfg.SCALE),
                    np.float32(0.0)).astype(np.float32)
    common = {
        "ffn_w": np.ascontiguousarray(inputs["L0_ffn_weight"], np.float32),
        "gamma2d": gamma2d, "conv_wfm": w_fm, "conv_bfm": b_fm,
        "rw_fm": rw_fm,
        "inproj": np.ascontiguousarray(inputs["L0_attn_in_proj_w"], np.float32),
        "chw0": np.ascontiguousarray(inputs["L0_attn_ch_gate_w0"], np.float32),
        "chw1r": chw1r,
        "chb1row": np.ascontiguousarray(
            np.asarray(inputs["L0_attn_ch_gate_b1"], np.float32).reshape(1, D)),
        "memw": np.ascontiguousarray(inputs["L0_attn_mem_gate_w"], np.float32),
        "memb2": memb2,
        "ident32": ident32, "mask": mask,
    }
    in_maps = []
    for c in range(n_cores):
        m = dict(common)
        m["x"] = np.ascontiguousarray(x[c * B:(c + 1) * B].reshape(B * L, D))
        in_maps.append(m)
    return in_maps


def kernel(**inputs):
    cfg = CFG()
    in_maps = make_in_maps(cfg, inputs, cfg.n_cores)
    nc = build(cfg)
    res = run_bass_kernel_spmd(nc, in_maps, core_ids=list(range(cfg.n_cores)))
    out = np.concatenate(
        [r["y"].reshape(cfg.B, cfg.L, cfg.D) for r in res.results], axis=0)
    return out.astype(np.float32)


# version 28 (final)
# speedup vs baseline: 1.0262x; 1.0262x over previous
import math
import numpy as np
from contextlib import ExitStack

import concourse.bass as bass
import concourse.mybir as mybir
from concourse.bass_utils import run_bass_kernel_spmd

F32 = mybir.dt.float32
F32R = mybir.dt.float32r
I32 = mybir.dt.int32
BF16 = mybir.dt.bfloat16
ALU = mybir.AluOpType
ACT = mybir.ActivationFunctionType
AX = mybir.AxisListType

LN_EPS = 1e-5
RMS_EPS = 1e-6
QP = 127.0
KNORM_EPS2 = 1e-24


class CFG:
    def __init__(self, D=2048, L=2048, B=2, M=64, RK=128, CH=128, MAC=256,
                 n_cores=8):
        self.D, self.L, self.B, self.M, self.RK = D, L, B, M, RK
        self.CH, self.MAC = CH, MAC
        self.DCH = D // 128
        self.NT = B * L // CH
        self.NTB = L // CH
        self.CPM = MAC // CH
        self.NMAC = L // MAC
        self.SCALE = 1.0 / math.sqrt(M)
        self.SEG = MAC + 3
        self.n_cores = n_cores


class Prog:
    ENGS = ("sync", "vector", "scalar", "tensor", "gpsimd")

    def __init__(self, nc):
        self.nc = nc
        self.streams = {e: [] for e in self.ENGS}
        self.cnt = {}
        self.sems = {}
        self.dirty = {e: set() for e in self.ENGS}
        self.dirty_r = {e: set() for e in self.ENGS}
        self.ctx = ExitStack()

    def sem(self, name):
        self.sems[name] = self.ctx.enter_context(self.nc.semaphore(name))
        self.cnt[name] = 0

    @staticmethod
    def _clash(aset, bset):
        for a in aset:
            an, _, asub = a.partition("#")
            for b in bset:
                bn, _, bsub = b.partition("#")
                if an == bn and (not asub or not bsub or asub == bsub):
                    return True
        return False

    def op(self, eng, fn, reads=(), writes=(), inc=None):
        if eng != "sync":
            r, w = set(reads), set(writes)
            if (self._clash(r, self.dirty[eng])
                    or self._clash(w, self.dirty[eng])
                    or self._clash(w, self.dirty_r[eng])):
                self.streams[eng].append(lambda e: e.drain())
                self.dirty[eng] = set()
                self.dirty_r[eng] = set()
        if inc is not None:
            sem = self.sems[inc]
            amt = 16 if inc.startswith("d_") else 1
            self.cnt[inc] += amt

            def emit(e, fn=fn, sem=sem, amt=amt):
                fn(e).then_inc(sem, amt)
            self.streams[eng].append(emit)
        else:
            self.streams[eng].append(fn)
        if eng != "sync":
            self.dirty[eng] |= set(writes)
            self.dirty_r[eng] |= set(reads)
        return self.cnt.get(inc)

    def wait(self, eng, sem_name, val):
        if val <= 0:
            return
        s = self.sems[sem_name]
        self.streams[eng].append(lambda e, s=s, v=val: e.wait_ge(s, v))

    def emit_all(self):
        with self.nc.Block() as blk:
            streams = self.streams

            @blk.sync
            def _(e):
                for fn in streams["sync"]:
                    fn(e)

            @blk.vector
            def _(e):
                for fn in streams["vector"]:
                    fn(e)

            @blk.scalar
            def _(e):
                for fn in streams["scalar"]:
                    fn(e)

            @blk.tensor
            def _(e):
                for fn in streams["tensor"]:
                    fn(e)

            @blk.gpsimd
            def _(e):
                for fn in streams["gpsimd"]:
                    fn(e)


def build(cfg: CFG):
    nc = bass.Bass("TRN2", target_bir_lowering=False,
                   dynamic_dma_scratch_size=8192)
    D, L, B, M, RK = cfg.D, cfg.L, cfg.B, cfg.M, cfg.RK
    CH, MAC, DCH, NT, NTB, CPM, NMAC, SEG = (cfg.CH, cfg.MAC, cfg.DCH, cfg.NT,
                                             cfg.NTB, cfg.CPM, cfg.NMAC,
                                             cfg.SEG)
    LNS = math.log(cfg.SCALE)
    NB = D // 512
    NG = max(1, DCH // 4)
    H = CH // 2

    x_in = nc.dram_tensor("x", [B * L, D], F32, kind="ExternalInput").ap()
    y_out = nc.dram_tensor("y", [B * L, D], F32, kind="ExternalOutput").ap()
    ffn_w = nc.dram_tensor("ffn_w", [D, D], F32, kind="ExternalInput").ap()
    gamma2d = nc.dram_tensor("gamma2d", [DCH, 128], F32, kind="ExternalInput").ap()
    conv_wfm = nc.dram_tensor("conv_wfm", [128, DCH * 4], F32, kind="ExternalInput").ap()
    conv_bfm = nc.dram_tensor("conv_bfm", [128, DCH], F32, kind="ExternalInput").ap()
    rw_fm_in = nc.dram_tensor("rw_fm", [128, DCH], F32, kind="ExternalInput").ap()
    inproj = nc.dram_tensor("inproj", [RK, D], F32, kind="ExternalInput").ap()
    chw0 = nc.dram_tensor("chw0", [RK, D], F32, kind="ExternalInput").ap()
    chw1r = nc.dram_tensor("chw1r", [128, D], F32, kind="ExternalInput").ap()
    chb1row = nc.dram_tensor("chb1row", [1, D], F32, kind="ExternalInput").ap()
    memw = nc.dram_tensor("memw", [RK, D], F32, kind="ExternalInput").ap()
    memb2 = nc.dram_tensor("memb2", [M, 2], F32, kind="ExternalInput").ap()
    ident32_in = nc.dram_tensor("ident32", [128, 128], F32, kind="ExternalInput").ap()
    mask_in = nc.dram_tensor("mask", [128, 128], F32, kind="ExternalInput").ap()

    u1_dram = nc.dram_tensor("u1_dram", [NT, DCH * 128 * 128], F32)
    u3_dram = nc.dram_tensor("u3_dram", [NT, DCH * 128 * 128], F32)
    x2_dram = nc.dram_tensor("x2_dram", [NT * 128, D], F32)
    cc1_in = nc.dram_tensor("cc1_in", [1, 128], F32)
    cc1_out = nc.dram_tensor("cc1_out", [1, 128], F32)
    cc2_in = nc.dram_tensor("cc2_in", [1, 128], F32)
    cc2_out = nc.dram_tensor("cc2_out", [1, 128], F32)
    scr_dram = nc.dram_tensor("scr_dram", [1, 16], F32)

    p = Prog(nc)
    ctx = p.ctx
    sb = lambda n, sh, dt=F32: ctx.enter_context(nc.sbuf_tensor(n, sh, dt))

    WqT = sb("WqT", [128, DCH * D], BF16)
    qbuf = sb("qbuf", [128, D])
    xbuf = sb("xbuf", [128, D])
    xqT = sb("xqT", [128, 2 * D], BF16)
    x1mac = sb("x1mac", [128, CPM * D])
    xl2c = sb("xl2c", [128, D])
    xl2T = sb("xl2T", [128, DCH * SEG])
    shb = sb("shb", [128, DCH * MAC])
    ch0sb = sb("ch0sb", [128, MAC], BF16)
    WinT = sb("WinT", [128, DCH * RK])
    WmgT = sb("WmgT", [128, DCH * RK])
    Wch0T = sb("Wch0T", [128, DCH * RK])
    W1T = sb("W1T", [RK, D], BF16)
    Ubuf = sb("Ubuf", [128, D], BF16)
    Ssb = sb("Ssb", [M, D])
    u3buf = sb("u3buf", [128, D])
    b1row = sb("b1row_sb", [1, D], BF16)
    onesbf = sb("onesbf", [1, 128], BF16)
    mvb = sb("mvb", [128, 12 * max(1, D // 512)])
    mv = sb("mv", [128, 4])
    acol = sb("acol", [128, 2])
    nmacol = sb("nmacol", [128, 2])
    tcol = sb("tcol", [128, 4])
    gam_fm = sb("gam_fm", [128, DCH])
    gamabs = sb("gamabs", [128, DCH])
    gamc1 = sb("gamc1", [128, DCH])
    gamc1b = sb("gamc1b", [128, DCH])
    cmrun = sb("cmrun", [128, DCH])
    cm3run = sb("cm3run", [128, DCH])
    tmp4 = sb("tmp4", [128, 8])
    cmcol = sb("cmcol", [128, 1])
    wsum = sb("wsum", [128, DCH])
    wscol = sb("wscol", [128, 1])
    onecol = sb("onecol", [128, 1])
    ones1r = sb("ones1r", [1, 128])
    sc1 = sb("sc1", [1, 16])
    c1col = sb("c1col", [128, 1])
    cecol = sb("cecol", [128, 1])
    c1bcol = sb("c1bcol", [128, 1])
    cebcol = sb("cebcol", [128, 1])
    ident32 = sb("ident32_sb", [128, 128])
    maskt = sb("maskt_sb", [128, 128])
    w_fm = sb("w_fm_sb", [128, DCH * 4])
    b_fm = sb("b_fm_sb", [128, DCH])
    rw_fm = sb("rw_fm_sb", [128, DCH])
    membc = sb("membc", [M, 2])
    m2a2 = sb("m2a2", [128, 2 * CPM])
    zeros64 = sb("zeros64", [M, CH])
    zhalo = sb("zhalo", [128, DCH * 3])
    selb = sb("selb", [M, MAC])
    taub = sb("taub", [M, MAC])
    gprod = sb("gprod", [M, MAC])
    rk1 = sb("rk1", [M, MAC])
    rk2 = sb("rk2", [M, MAC])
    cum = sb("cum", [M, MAC])
    rt1 = sb("rt1", [M, MAC])
    e1t = sb("e1t", [M, MAC])
    kq = sb("kq", [M, MAC])
    itg = sb("itg", [M, MAC])
    e2t = sb("e2t", [M, MAC])
    qt0 = sb("qt0", [M, CH])
    qtb = sb("qtb", [M, 3 * CH])
    kt0 = sb("kt0", [M, CH])
    khatT = sb("khatT", [128, M], BF16)
    PTsb = sb("PTsb", [128, 128], BF16)
    knrow = sb("knrow", [1, 2 * 128])
    edec = sb("edec", [M, 2])
    lnscol = sb("lnscol", [M, 1])
    nref = sb("nref", [M, 16])

    PS = ctx.enter_context(nc.psum_tensor("PS", [128, 4096], F32))

    for s in ("d_ldx", "d_ldu", "d_ldw", "d_stu", "d_stx2", "d_sty", "d_g",
              "c_v", "c_s", "c_t", "c_g", "c_cc"):
        p.sem(s)
    Wt = p.wait
    O = p.op
    C = p.cnt

    def rsqrt_chain(vcol_ap, out_ap, mul, add, in_name, out_name, par=0):
        O("vector", lambda e, par=par: e.tensor_scalar(
            tcol[:, 2 * par:2 * par + 1], vcol_ap, float(mul), float(add),
            ALU.mult, ALU.add),
          reads=(in_name,), writes=("tcol#%d" % par,), inc="c_v")
        Wt("scalar", "c_v", C["c_v"])
        O("scalar", lambda e, par=par: e.activation(
            tcol[:, 2 * par + 1:2 * par + 2], tcol[:, 2 * par:2 * par + 1],
            ACT.Ln),
          reads=("tcol#%d" % par,), writes=("tcol#%d" % par,))
        O("scalar", lambda e, par=par: e.activation(
            out_ap, tcol[:, 2 * par + 1:2 * par + 2], ACT.Exp, scale=-0.5),
          reads=("tcol#%d" % par,), writes=(out_name,), inc="c_s")
        return C["c_s"]

    bank_free = {4: ("c_v", 0), 5: ("c_v", 0)}

    def bank_cols(bk):
        return 2048 + 512 * (bk - 4)

    O("sync", lambda e: e.dma_start(out=ident32[:, :], in_=ident32_in), inc="d_ldw")
    O("sync", lambda e: e.dma_start(out=maskt[:, :], in_=mask_in), inc="d_ldw")
    O("sync", lambda e: e.dma_start(out=w_fm[:, :], in_=conv_wfm), inc="d_ldw")
    O("sync", lambda e: e.dma_start(out=b_fm[:, :], in_=conv_bfm), inc="d_ldw")
    O("sync", lambda e: e.dma_start(out=rw_fm[:, :], in_=rw_fm_in), inc="d_ldw")
    O("sync", lambda e: e.dma_start(out=membc[:, :], in_=memb2), inc="d_ldw")
    O("gpsimd", lambda e: e.dma_start(out=b1row[:, :], in_=chb1row), inc="d_g")
    O("sync", lambda e: e.dma_start(out=xbuf[0:DCH, 0:128], in_=gamma2d), inc="d_ldw")
    ld0 = C["d_ldw"]

    Wt("vector", "d_ldw", ld0)
    O("vector", lambda e: e.memset(PS[:, :], 0.0), writes=("PS",))
    O("vector", lambda e: e.memset(zeros64[:, :], 0.0), writes=("zeros64",))
    O("vector", lambda e: e.memset(onecol[:, :], 1.0), writes=("onecol",))
    O("vector", lambda e: e.memset(ones1r[:, :], 1.0), writes=("ones1r",))
    O("vector", lambda e: e.memset(onesbf[:, :], 1.0), writes=("onesbf",))
    O("vector", lambda e: e.memset(cmrun[:, :], 0.0), writes=("cmrun",))
    O("vector", lambda e: e.memset(cm3run[:, :], 0.0), writes=("cm3run",))
    O("vector", lambda e: e.memset(zhalo[:, :], 0.0), writes=("zhalo",))
    O("vector", lambda e: e.memset(lnscol[:, :], LNS), writes=("lnscol",),
      inc="c_v")
    vz = C["c_v"]

    Wt("tensor", "d_ldw", ld0)
    Wt("tensor", "c_v", vz)
    O("tensor", lambda e: e.transpose(PS[:, 2048:2048 + DCH],
                                      xbuf[0:DCH, 0:128],
                                      ident32[0:DCH, 0:DCH]), inc="c_t")
    xstage_free = C["c_t"]
    Wt("scalar", "c_t", C["c_t"])
    O("scalar", lambda e: e.copy(gam_fm[:, :], PS[:, 2048:2048 + DCH]),
      writes=("gam_fm",))
    O("scalar", lambda e: e.activation(gamabs[:, :], gam_fm[:, :], ACT.Abs),
      reads=("gam_fm",), writes=("gamabs",), inc="c_s")
    gam_done = C["c_s"]
    bank_free[4] = ("c_s", gam_done)

    def wprep(dram_ap, dst, dst_name, to_f32r):
        nonlocal xstage_free
        Wt("sync", "c_t", xstage_free)
        ldc = O("sync", lambda e: e.dma_start(out=xbuf[:, 0:D], in_=dram_ap),
                inc="d_ldw")
        Wt("tensor", "d_ldw", ldc)
        last = 0
        for c in range(DCH):
            bk = 4 + (c % 2)
            sname, sval = bank_free[bk]
            Wt("tensor", sname, sval)
            O("tensor", lambda e, c=c, bk=bk: e.transpose(
                PS[:, bank_cols(bk):bank_cols(bk) + 128],
                xbuf[:, c * 128:(c + 1) * 128], ident32[:, :]), inc="c_t")
            Wt("vector", "c_t", C["c_t"])
            O("vector", lambda e, c=c, bk=bk: e.tensor_copy(
                dst[:, c * 128:(c + 1) * 128].bitcast(F32R) if to_f32r
                else dst[:, c * 128:(c + 1) * 128],
                PS[:, bank_cols(bk):bank_cols(bk) + 128]),
              writes=(dst_name,), inc="c_v")
            last = C["c_v"]
            bank_free[bk] = ("c_v", last)
        xstage_free = C["c_t"]
        return last

    wprep(inproj, WinT, "WinT", True)
    wprep(memw, WmgT, "WmgT", True)
    wprep(chw0, Wch0T, "Wch0T", True)
    w1_done = wprep(chw1r, W1T, "W1T", False)

    wstage = x1mac[:, 0:D]
    wqstage = x1mac[:, D:2 * D]
    prev_red = 0
    for c in range(DCH):
        if c > 0:
            Wt("sync", "c_v", prev_red)
        else:
            Wt("sync", "c_v", w1_done)
        ldc = O("sync", lambda e, c=c: e.dma_start(
            out=wstage, in_=ffn_w[c * 128:(c + 1) * 128, :]), inc="d_ldw")
        Wt("vector", "d_ldw", ldc)
        O("vector", lambda e, c=c: e.tensor_reduce(
            wsum[:, c:c + 1], wstage, AX.X, ALU.add,
            apply_absolute_value=True), reads=("x1mac",), writes=("wsum",),
          inc="c_v")
        prev_red = C["c_v"]
    O("vector", lambda e: e.tensor_reduce(wscol[:, :], wsum[:, :], AX.X,
                                          ALU.add),
      reads=("wsum",), writes=("wscol",), inc="c_v")
    Wt("tensor", "c_v", C["c_v"])
    O("tensor", lambda e: e.matmul(PS[0:1, 3072:3073], onecol[:, :],
                                   wscol[:, :], start=True, stop=True),
      inc="c_t")
    Wt("vector", "c_t", C["c_t"])
    O("vector", lambda e: e.tensor_scalar(
        sc1[0:1, 0:1], PS[0:1, 3072:3073], 1.0 / (D * D), None, ALU.mult),
      writes=("sc1",))
    O("vector", lambda e: e.reciprocal(sc1[0:1, 1:2], sc1[0:1, 0:1]),
      reads=("sc1",), writes=("sc1",), inc="c_v")
    wsv = C["c_v"]
    O("gpsimd", lambda e, v=wsv: e.wait_ge(p.sems["c_v"], v))
    O("gpsimd", lambda e: e.dma_start(out=scr_dram.ap()[0:1, 0:2],
                                      in_=sc1[0:1, 0:2]), inc="d_g")
    O("gpsimd", lambda e, v=C["d_g"]: e.wait_ge(p.sems["d_g"], v))
    O("gpsimd", lambda e: e.dma_start(
        out=wscol[:, 0:1], in_=scr_dram.ap()[0:1, 1:2].broadcast_to((128, 1))),
      inc="d_g")
    g_ws = C["d_g"]

    Wt("vector", "d_g", g_ws)
    wst = [x1mac[:, 0:D], x1mac[:, D:2 * D]]
    wqs = [xl2c[:, :], u3buf[:, :]]
    wqs_nm = ["xl2c", "u3buf"]
    w_ld = [0] * DCH
    w_free = [0, 0]
    w_tr = [0, 0]
    Wt("sync", "d_ldw", C["d_ldw"])
    Wt("sync", "c_v", prev_red)
    for c in range(2):
        O("sync", lambda e, c=c, par=c % 2: e.dma_start(
            out=wst[par], in_=ffn_w[c * 128:(c + 1) * 128, :]), inc="d_ldw")
        w_ld[c] = C["d_ldw"]
        Wt("sync", "d_ldw", w_ld[c])
    for c in range(DCH):
        par = c % 2
        Wt("vector", "d_ldw", w_ld[c])
        O("vector", lambda e, par=par: e.tensor_scalar(
            qbuf[:, :], wst[par], wscol[:, 0:1], 1.0, ALU.mult, ALU.min),
          reads=("x1mac#%d" % par, "wscol"), writes=("qbuf",), inc="c_v")
        w_free[par] = C["c_v"]
        if c + 2 < DCH:
            Wt("sync", "c_v", w_free[par])
            O("sync", lambda e, c=c, par=par: e.dma_start(
                out=wst[par], in_=ffn_w[(c + 2) * 128:(c + 3) * 128, :]),
              inc="d_ldw")
            w_ld[c + 2] = C["d_ldw"]
            Wt("sync", "d_ldw", w_ld[c + 2])
        O("vector", lambda e: e.tensor_scalar(
            qbuf[:, :], qbuf[:, :], -1.0, None, ALU.max),
          reads=("qbuf",), writes=("qbuf",))
        O("vector", lambda e: e.tensor_scalar(
            qbuf[:, :], qbuf[:, :], 12582912.0, None, ALU.add),
          reads=("qbuf",), writes=("qbuf",))
        if c >= 2:
            Wt("vector", "c_t", w_tr[par])
        O("vector", lambda e, par=par: e.tensor_scalar(
            wqs[par], qbuf[:, :], -12582912.0, None, ALU.add),
          reads=("qbuf",), writes=(wqs_nm[par],), inc="c_v")
        w_vq = C["c_v"]
        Wt("tensor", "c_v", w_vq)
        for t in range(DCH):
            bk = 4 + (t % 2)
            sname, sval = bank_free[bk]
            Wt("tensor", sname, sval)
            O("tensor", lambda e, t=t, bk=bk, par=par: e.transpose(
                PS[:, bank_cols(bk):bank_cols(bk) + 128],
                wqs[par][:, t * 128:(t + 1) * 128], ident32[:, :]), inc="c_t")
            Wt("scalar", "c_t", C["c_t"])
            O("scalar", lambda e, t=t, c=c, bk=bk: e.copy(
                WqT[:, t * D + c * 128:t * D + (c + 1) * 128],
                PS[:, bank_cols(bk):bank_cols(bk) + 128]),
              writes=("WqT",), inc="c_s")
            bank_free[bk] = ("c_s", C["c_s"])
        w_tr[par] = C["c_t"]

    wprep_t = C["c_t"]
    wprep_g = C["c_v"]

    prev_vact = [0, 0]
    prev_spill = 0
    prev_red1 = 0
    prev_tr_last = 0
    xst = [xbuf[:, :], x1mac[:, 0:D]]
    xst_names = ["xbuf", "x1mac"]
    p1_ld = [0] * NT
    p1_sv = [0] * NT
    ngrp = max(1, D // 512)

    def p1_load(t):
        par = t % 2
        if t >= 2:
            Wt("sync", "c_v", prev_vact[par])
        elif t == 1:
            Wt("sync", "d_ldx", p1_ld[0])
            Wt("sync", "c_v", w_free[0])
        O("sync", lambda e, t=t, par=par: e.dma_start(
            out=xst[par], in_=x_in[t * 128:(t + 1) * 128, :]), inc="d_ldx")
        p1_ld[t] = C["d_ldx"]

    def p1_stats(t):
        par = t % 2
        nm = xst_names[par]
        Wt("vector", "d_ldx", p1_ld[t])
        for g in range(ngrp):
            O("vector", lambda e, g=g, par=par: e.bn_stats(
                mvb[:, 6 * (par * ngrp + g):6 * (par * ngrp + g + 1)],
                xst[par][:, 512 * g:512 * (g + 1)]),
              reads=(nm,), writes=("mvb#%d" % par,))
        O("vector", lambda e, par=par: e.bn_aggr(
            mv[:, 2 * par:2 * par + 2],
            mvb[:, 6 * par * ngrp:6 * (par + 1) * ngrp]),
          reads=("mvb#%d" % par,), writes=("mv#%d" % par,), inc="c_v")
        rsqrt_chain(mv[:, 2 * par + 1:2 * par + 2],
                    acol[:, par:par + 1], 1.0 + RMS_EPS, RMS_EPS * LN_EPS,
                    "mv#%d" % par, "acol#%d" % par, par=par)
        Wt("vector", "c_s", C["c_s"])
        O("vector", lambda e, par=par: e.tensor_scalar(
            nmacol[:, par:par + 1], mv[:, 2 * par:2 * par + 1],
            acol[:, par:par + 1], -1.0, ALU.mult, ALU.mult),
          reads=("mv#%d" % par, "acol#%d" % par),
          writes=("nmacol#%d" % par,), inc="c_v")
        p1_sv[t] = C["c_v"]

    p1_load(0)
    p1_stats(0)
    for t in range(NT):
        par = t % 2
        if t + 1 < NT:
            p1_load(t + 1)
            p1_stats(t + 1)
        if t > 0:
            Wt("vector", "c_t", prev_tr_last)
        O("vector", lambda e, par=par: e.tensor_scalar(
            qbuf[:, :], xst[par], acol[:, par:par + 1],
            nmacol[:, par:par + 1], ALU.mult, ALU.add),
          reads=(xst_names[par], "nmacol#%d" % par, "acol#%d" % par),
          writes=("qbuf",), inc="c_v")
        prev_vact[par] = C["c_v"]
        Wt("tensor", "c_v", prev_vact[par])
        for g in range(NG):
            bk = 4 + (g % 2)
            sname, sval = bank_free[bk]
            Wt("tensor", sname, sval)
            nch = min(4, DCH - g * 4)
            for j in range(nch):
                c = g * 4 + j
                O("tensor", lambda e, c=c, bk=bk, j=j: e.transpose(
                    PS[:, bank_cols(bk) + j * 128:bank_cols(bk) + (j + 1) * 128],
                    qbuf[:, c * 128:(c + 1) * 128], ident32[:, :]), inc="c_t")
            prev_tr_last = C["c_t"]
            Wt("scalar", "c_t", prev_tr_last)
            if t > 0 and g == 0:
                Wt("scalar", "d_stu", prev_spill)
                Wt("scalar", "c_v", prev_red1)
            O("scalar", lambda e, g=g, bk=bk, nch=nch: e.copy(
                u3buf[:, g * 512:g * 512 + nch * 128],
                PS[:, bank_cols(bk):bank_cols(bk) + nch * 128]),
              writes=("u3buf#%d" % g,), inc="c_s")
            ev = C["c_s"]
            bank_free[bk] = ("c_s", ev)
            Wt("vector", "c_s", ev)
            O("vector", lambda e, g=g, nch=nch: e.tensor_reduce(
                tmp4[:, (g % 2) * 4:(g % 2) * 4 + nch],
                u3buf[:, g * 512:g * 512 + nch * 128].rearrange(
                    "p (c f) -> p c f", f=128),
                AX.X, ALU.max, apply_absolute_value=True),
              reads=("u3buf#%d" % g,), writes=("tmp4#%d" % (g % 2),))
            O("vector", lambda e, g=g, nch=nch: e.tensor_tensor(
                cmrun[:, g * 4:g * 4 + nch], cmrun[:, g * 4:g * 4 + nch],
                tmp4[:, (g % 2) * 4:(g % 2) * 4 + nch], ALU.max),
              reads=("tmp4#%d" % (g % 2), "cmrun#%d" % g),
              writes=("cmrun#%d" % g,), inc="c_v")
            prev_red1 = C["c_v"]
            Wt("scalar", "c_s", ev)
            O("scalar", lambda e, t=t, g=g, nch=nch: e.dma_start(
                out=u1_dram.ap()[t:t + 1,
                                 g * 4 * 128 * 128:(g * 4 + nch) * 128 * 128]
                .rearrange("one (c p f) -> (one p) c f", p=128, f=128),
                in_=u3buf[:, g * 512:g * 512 + nch * 128]), inc="d_stu")
            prev_spill = C["d_stu"]
    ph1_tr_last = prev_tr_last
    ph1_spill = prev_spill
    ph1_red = prev_red1

    def allreduce_scale(cmr_name, cmr, cc_in_t, cc_out_t, sidx, c1c, cec,
                        gamdst, gamdst_name):
        O("vector", lambda e: e.tensor_tensor(cmr[:, :], cmr[:, :],
                                              gamabs[:, :], ALU.mult),
          reads=(cmr_name, "gamabs"), writes=(cmr_name,))
        O("vector", lambda e: e.tensor_reduce(cmcol[:, :], cmr[:, :], AX.X,
                                              ALU.max),
          reads=(cmr_name,), writes=("cmcol",), inc="c_v")
        v0 = C["c_v"]
        O("gpsimd", lambda e, v=v0: e.wait_ge(p.sems["c_v"], v))
        O("gpsimd", lambda e: e.dma_start(
            out=cc_in_t.ap().rearrange("one (p f) -> (one p) f", f=1),
            in_=cmcol[:, 0:1]), inc="d_g")
        O("gpsimd", lambda e, v=C["d_g"]: e.wait_ge(p.sems["d_g"], v))
        O("gpsimd", lambda e: e.collective_compute(
            "AllReduce", ALU.max, replica_groups=[list(range(cfg.n_cores))],
            ins=[cc_in_t.ap()], outs=[cc_out_t.ap()]), inc="c_cc")
        O("gpsimd", lambda e, v=C["c_cc"]: e.wait_ge(p.sems["c_cc"], v))
        O("gpsimd", lambda e, v=C["d_stu"]: e.wait_ge(p.sems["d_stu"], v))
        O("gpsimd", lambda e: e.dma_start(out=u3buf[0:1, 0:128],
                                          in_=cc_out_t.ap()), inc="d_g")
        gv = C["d_g"]
        Wt("vector", "d_g", gv)
        O("vector", lambda e, sidx=sidx: e.tensor_reduce(
            sc1[0:1, sidx:sidx + 1], u3buf[0:1, 0:128], AX.X, ALU.max),
          reads=("u3buf",), writes=("sc1",))
        O("vector", lambda e, sidx=sidx: e.reciprocal(
            sc1[0:1, sidx + 1:sidx + 2], sc1[0:1, sidx:sidx + 1]),
          reads=("sc1",), writes=("sc1",))
        O("vector", lambda e, sidx=sidx: e.tensor_scalar(
            sc1[0:1, sidx + 2:sidx + 3], sc1[0:1, sidx + 1:sidx + 2], QP,
            None, ALU.mult), reads=("sc1",), writes=("sc1",))
        O("vector", lambda e, sidx=sidx: e.tensor_scalar(
            sc1[0:1, sidx + 3:sidx + 4], sc1[0:1, sidx:sidx + 1],
            sc1[0:1, 0:1], 0.5 / QP, ALU.mult, ALU.mult),
          reads=("sc1",), writes=("sc1",), inc="c_v")
        v1 = C["c_v"]
        O("gpsimd", lambda e, v=v1: e.wait_ge(p.sems["c_v"], v))
        O("gpsimd", lambda e, sidx=sidx: e.dma_start(
            out=scr_dram.ap()[0:1, sidx + 2:sidx + 4],
            in_=sc1[0:1, sidx + 2:sidx + 4]), inc="d_g")
        O("gpsimd", lambda e, v=C["d_g"]: e.wait_ge(p.sems["d_g"], v))
        O("gpsimd", lambda e, sidx=sidx: e.dma_start(
            out=c1c[:, 0:1],
            in_=scr_dram.ap()[0:1, sidx + 2:sidx + 3].broadcast_to((128, 1))),
          inc="d_g")
        O("gpsimd", lambda e, sidx=sidx: e.dma_start(
            out=cec[:, 0:1],
            in_=scr_dram.ap()[0:1, sidx + 3:sidx + 4].broadcast_to((128, 1))),
          inc="d_g")
        gv2 = C["d_g"]
        Wt("vector", "d_g", gv2)
        O("vector", lambda e: e.tensor_scalar(
            gamdst[:, :], gam_fm[:, :], c1c[:, 0:1], None, ALU.mult),
          reads=("gam_fm",), writes=(gamdst_name,), inc="c_v")
        return C["c_v"]

    ph2_ready = allreduce_scale("cmrun", cmrun, cc1_in, cc1_out, 4,
                                c1col, cecol, gamc1, "gamc1")
    ph1_red = ph2_ready
    b1_g = C["d_g"]

    b03_free = ("c_v", 0)
    last_qb = 0
    last_xq_any = C["c_v"]
    last_ffn = [0, 0]
    last_x1 = 0
    xbuf_free = ("c_v", 0)
    last_xl2ready = 0
    last_conv = 0
    last_o_t = wprep_t
    last_proj_t = 0
    last_sh2 = 0
    last_dsdec = 0
    last_PT_ev = 0
    last_khatT_ev = 0
    last_Ut = 0
    last_u3_spill = ph1_spill
    last_u3_red = ph1_red
    last_x2_spill = 0
    last_x2b = 0
    last_v3 = 0
    last_xl2re = 0
    last_LN3_tr = ph1_tr_last
    last_gprod = 0
    last_kq = 0

    Wt("sync", "c_v", ph2_ready)

    hoist = {}
    qstate = {"last_xq_any": last_xq_any, "last_ffn": last_ffn,
              "xl2c_free_t": last_LN3_tr,
              "u3s": C["c_s"], "u3d": C["d_stu"]}

    def emit_quant_load(tg, dram, nm):
        Wt("sync", "c_v", qstate["last_xq_any"])
        ldu = [0] * 4
        for gq in range(4):
            O("sync", lambda e, tg=tg, gq=gq, dram=dram: e.dma_start(
                out=qbuf[:, gq * 512:(gq + 1) * 512],
                in_=dram.ap()[tg:tg + 1,
                              gq * 4 * 128 * 128:(gq + 1) * 4 * 128 * 128]
                .rearrange("one (c p f) -> (one p) c f", p=128, f=128)),
              inc="d_ldu")
            ldu[gq] = C["d_ldu"]
            Wt("sync", "d_ldu", ldu[gq])
        return ldu

    def emit_quant_scale(gq, ldu, gsc, gnm):
        Wt("vector", "d_ldu", ldu[gq])
        for c in range(gq * 4, gq * 4 + 4):
            O("vector", lambda e, c=c, gsc=gsc: e.tensor_scalar(
                qbuf[:, c * 128:(c + 1) * 128],
                qbuf[:, c * 128:(c + 1) * 128],
                gsc[:, c:c + 1], None, ALU.mult),
              reads=("qbuf#%d" % c, gnm), writes=("qbuf#%d" % c,))

    def emit_quant_round(tg):
        kk = tg % 2
        O("vector", lambda e: e.tensor_scalar(
            qbuf[:, :], qbuf[:, :], 12582912.0, None, ALU.add),
          reads=("qbuf",), writes=("qbuf",))
        if tg > 1:
            Wt("vector", "c_t", qstate["last_ffn"][kk])
        O("vector", lambda e, kk=kk: e.tensor_scalar(
            xqT[:, kk * D:(kk + 1) * D], qbuf[:, :], -12582912.0, None,
            ALU.add),
          reads=("qbuf",), writes=("xqT",), inc="c_v")
        qstate["last_xq_any"] = C["c_v"]

    def emit_quant(tg):
        ldu = emit_quant_load(tg, u1_dram, "gamc1")
        for gq in range(4):
            emit_quant_scale(gq, ldu, gamc1, "gamc1")
        emit_quant_round(tg)

    emit_quant(0)
    for b in range(B):
        O("vector", lambda e: e.tensor_copy(
            xl2T[:, :].bitcast(F32R).rearrange(
                "p (c s) -> p c s", s=SEG)[:, :, 0:3],
            zhalo[:, :].rearrange("p (c s) -> p c s", s=3)),
          reads=("zhalo",), writes=("xl2T",), inc="c_v")
        zh = C["c_v"]
        for mi in range(NMAC):
            segs = xl2T[:, :].rearrange("p (c s) -> p c s", s=SEG)
            accs = shb[:, :].rearrange("p (c s) -> p c s", s=MAC)
            shr = shb[:, :].rearrange("p (c s) -> p c s", s=MAC)
            shrr = shb[:, :].bitcast(F32R).rearrange("p (c s) -> p c s",
                                                     s=MAC)
            segr = xl2T[:, :].bitcast(F32R).rearrange("p (c s) -> p c s",
                                                      s=SEG)
            silg = [0] * NG
            convg = [0] * NG
            for k in range(CPM):
                tg = b * NTB + mi * CPM + k
                kk = tg % 2
                if hoist.get("tg") == tg:
                    ldx = hoist["ldx"]
                    ffn_bk = hoist["ffn_bk"]
                    last_ffn[kk] = ffn_bk[NB - 1]
                else:
                    sn, sv = xbuf_free
                    Wt("sync", sn, sv)
                    O("sync", lambda e, tg=tg: e.dma_start(
                        out=xbuf[:, :], in_=x_in[tg * 128:(tg + 1) * 128, :]),
                      inc="d_ldx")
                    ldx = C["d_ldx"]
                    Wt("tensor", "c_v", qstate["last_xq_any"])
                    sn, sv = b03_free
                    Wt("tensor", sn, sv)
                    ffn_bk = [0] * NB
                    for nb in range(NB):
                        for c in range(DCH):
                            O("tensor", lambda e, c=c, nb=nb, kk=kk: e.matmul(
                                PS[:, nb * 512:(nb + 1) * 512],
                                xqT[:, kk * D + c * 128:
                                    kk * D + (c + 1) * 128],
                                WqT[:, c * D + nb * 512:
                                    c * D + (nb + 1) * 512],
                                start=(c == 0), stop=(c == DCH - 1)),
                              inc="c_t" if c == DCH - 1 else None)
                        ffn_bk[nb] = C["c_t"]
                    last_ffn[kk] = C["c_t"]
                do_q = tg + 1 < NT
                if do_q:
                    qldu = emit_quant_load(tg + 1, u1_dram, "gamc1")
                Wt("vector", "d_ldx", ldx)
                if mi > 0 or b > 0:
                    Wt("vector", "d_stx2", last_x2_spill)
                for nb in range(NB):
                    if do_q:
                        emit_quant_scale(nb, qldu, gamc1, "gamc1")
                    Wt("vector", "c_t", ffn_bk[nb])
                    O("vector", lambda e, k=k, nb=nb: e.scalar_tensor_tensor(
                        x1mac[:, k * D + nb * 512:k * D + (nb + 1) * 512],
                        PS[:, nb * 512:(nb + 1) * 512], cecol[:, 0:1],
                        xbuf[:, nb * 512:(nb + 1) * 512], ALU.mult, ALU.add),
                      reads=("xbuf", "cecol"), writes=("x1mac#b%d" % nb,),
                      inc="c_v" if nb == NB - 1 else None)
                    O("vector", lambda e, k=k, nb=nb: e.bn_stats(
                        mvb[:, 6 * nb:6 * (nb + 1)],
                        x1mac[:, k * D + 512 * nb:k * D + 512 * (nb + 1)]),
                      reads=("x1mac#b%d" % nb,), writes=("mvb#%d" % nb,))
                last_x1 = C["c_v"]
                b03_free = ("c_v", last_x1)
                xbuf_free = ("c_v", last_x1)
                O("vector", lambda e: e.bn_aggr(mv[:, 0:2], mvb[:, 0:24]),
                  reads=("mvb",), writes=("mv",), inc="c_v")
                rsqrt_chain(mv[:, 1:2], m2a2[:, 2 * k + 1:2 * k + 2], 1.0,
                            LN_EPS, "mv", "m2a2")
                Wt("vector", "c_s", C["c_s"])
                O("vector", lambda e, k=k: e.tensor_scalar(
                    m2a2[:, 2 * k:2 * k + 1], mv[:, 0:1],
                    m2a2[:, 2 * k + 1:2 * k + 2], -1.0, ALU.mult, ALU.mult),
                  reads=("mv", "m2a2"), writes=("m2a2",), inc="c_v")
                m2a2_v = C["c_v"]
                Wt("vector", "c_t", qstate["xl2c_free_t"])
                O("vector", lambda e, k=k: e.tensor_scalar(
                    xl2c[:, :], x1mac[:, k * D:(k + 1) * D],
                    m2a2[:, 2 * k + 1:2 * k + 2], m2a2[:, 2 * k:2 * k + 1],
                    ALU.mult, ALU.add),
                  reads=("x1mac", "m2a2"), writes=("xl2c",), inc="c_v")
                xl2_s = C["c_v"]
                if do_q:
                    emit_quant_round(tg + 1)
                Wt("tensor", "c_v", xl2_s)
                for g in range(NG):
                    bk = 4 + (g % 2)
                    sname, sval = bank_free[bk]
                    Wt("tensor", sname, sval)
                    if mi + k == 0 and g == 0:
                        Wt("tensor", "c_v", last_dsdec)
                    nch = min(4, DCH - g * 4)
                    for j in range(nch):
                        c = g * 4 + j
                        O("tensor", lambda e, c=c, bk=bk, j=j: e.transpose(
                            PS[:, bank_cols(bk) + j * 128:
                               bank_cols(bk) + (j + 1) * 128],
                            xl2c[:, c * 128:(c + 1) * 128], ident32[:, :]),
                          inc="c_t")
                    tr = C["c_t"]
                    if g == NG - 1:
                        last_LN3_tr = tr
                        qstate["xl2c_free_t"] = tr
                    Wt("scalar", "c_t", tr)
                    if mi + k == 0:
                        Wt("scalar", "c_v", zh)
                    if g == 0 and (mi > 0 or b > 0):
                        Wt("scalar", "c_v", last_conv)
                    O("scalar", lambda e, g=g, k=k, bk=bk, nch=nch: e.copy(
                        xl2T[:, :].bitcast(F32R).rearrange(
                            "p (c s) -> p c s", s=SEG)[
                            :, g * 4:g * 4 + nch, 3 + k * CH:3 + (k + 1) * CH],
                        PS[:, bank_cols(bk):bank_cols(bk) + nch * 128].rearrange(
                            "p (c f) -> p c f", f=128)),
                      writes=("xl2T",), inc="c_s")
                    ev_g = C["c_s"]
                    bank_free[bk] = ("c_s", ev_g)
                    if k == CPM - 1:
                        grp = range(g * 4, g * 4 + 4)
                        if g == 0:
                            Wt("tensor", "c_v", last_x1)
                            Wt("tensor", "c_v",
                               max(last_dsdec, last_kq, last_x2b))
                            Wt("vector", "c_t", last_o_t)
                        Wt("tensor", "c_s", ev_g)
                        for c in grp:
                            O("tensor", lambda e, c=c: e.matmul(
                                PS[0:RK, 0:MAC],
                                Wch0T[:, c * RK:(c + 1) * RK].bitcast(F32R),
                                segr[:, c, 3:3 + MAC],
                                start=(c == 0), stop=(c == DCH - 1)))
                        Wt("vector", "c_s", ev_g)
                        for c in grp:
                            O("vector", lambda e, c=c: e.tensor_scalar(
                                accs[:, c, :].bitcast(F32R),
                                segs[:, c, 0:MAC],
                                w_fm[:, 4 * c:4 * c + 1],
                                b_fm[:, c:c + 1], ALU.mult, ALU.add),
                              reads=("xl2T#%d" % c, "w_fm", "b_fm"),
                              writes=("shb#%d" % c,))
                            for i in range(1, 4):
                                O("vector", lambda e, c=c, i=i:
                                  e.scalar_tensor_tensor(
                                    accs[:, c, :].bitcast(F32R),
                                    segs[:, c, i:i + MAC],
                                    w_fm[:, 4 * c + i:4 * c + i + 1],
                                    accs[:, c, :], ALU.mult, ALU.add),
                                  reads=("xl2T#%d" % c, "w_fm",
                                         "shb#%d" % c),
                                  writes=("shb#%d" % c,))
                        O("vector", lambda e: e.engine_nop(), inc="c_v")
                        convg[g] = C["c_v"]
                        last_conv = convg[g]
            xl2_all = C["c_s"]
            for g in range(NG):
                Wt("scalar", "c_v", convg[g])
                O("scalar", lambda e, g=g: e.activation(
                    shb[:, g * 4 * MAC:(g + 1) * 4 * MAC].bitcast(F32R),
                    shb[:, g * 4 * MAC:(g + 1) * 4 * MAC], ACT.Silu),
                  reads=tuple("shb#%d" % c for c in range(g * 4, g * 4 + 4)),
                  writes=tuple("shb#%d" % c for c in range(g * 4, g * 4 + 4)),
                  inc="c_s")
                silg[g] = C["c_s"]

            for g in range(NG):
                Wt("tensor", "c_s", silg[g])
                for c in range(g * 4, g * 4 + 4):
                    st = (c == 0)
                    sp = (c == DCH - 1)
                    O("tensor", lambda e, c=c, st=st, sp=sp: e.matmul(
                        PS[0:M, 512:512 + MAC],
                        WinT[:, c * RK:c * RK + M].bitcast(F32R),
                        shrr[:, c, :], start=st, stop=sp))
                    O("tensor", lambda e, c=c, st=st, sp=sp: e.matmul(
                        PS[0:M, 1024:1024 + MAC],
                        WinT[:, c * RK + M:c * RK + RK].bitcast(F32R),
                        shrr[:, c, :], start=st, stop=sp))
                    O("tensor", lambda e, c=c, st=st, sp=sp: e.matmul(
                        PS[0:M, 3072:3072 + MAC],
                        WmgT[:, c * RK:c * RK + M].bitcast(F32R),
                        shrr[:, c, :], start=st, stop=sp))
                    O("tensor", lambda e, c=c, st=st, sp=sp: e.matmul(
                        PS[0:M, 3584:3584 + MAC],
                        WmgT[:, c * RK + M:c * RK + RK].bitcast(F32R),
                        shrr[:, c, :], start=st, stop=sp),
                      inc="c_t" if sp else None)
            O("vector", lambda e: e.tensor_copy(
                segs[:, :, 0:3].bitcast(F32R), segs[:, :, MAC:MAC + 3]),
              reads=("xl2T",), writes=("xl2T",), inc="c_v")
            last_proj_t = C["c_t"]
            Wt("scalar", "c_t", last_proj_t)
            O("scalar", lambda e: e.copy(rk1[:, :], PS[0:M, 512:512 + MAC]),
              writes=("rk1",))
            O("scalar", lambda e: e.copy(
                rk2[:, :], PS[0:M, 1024:1024 + MAC]),
              writes=("rk2",))
            O("scalar", lambda e: e.activation(
                selb[:, :], PS[0:M, 3072:3072 + MAC], ACT.Exp,
                bias=membc[:, 0:1]), reads=("membc",), writes=("selb",))
            O("scalar", lambda e: e.activation(
                selb[:, :], selb[:, :], ACT.Ln, bias=onecol[0:M, 0:1]),
              reads=("selb", "onecol"), writes=("selb",))
            O("scalar", lambda e: e.activation(
                taub[:, :], PS[0:M, 3584:3584 + MAC], ACT.Sigmoid,
                bias=membc[:, 1:2]), reads=("membc",), writes=("taub",))
            O("scalar", lambda e: e.copy(ch0sb[:, :], PS[:, 0:MAC]),
              writes=("ch0sb",), inc="c_s")
            last_projev = C["c_s"]
            bank_free[4] = ("c_s", last_projev)
            bank_free[5] = ("c_s", last_projev)
            Wt("vector", "c_s", last_projev)
            O("vector", lambda e: e.tensor_tensor(
                gprod[:, :], selb[:, :], taub[:, :], ALU.mult),
              reads=("selb", "taub"), writes=("gprod",), inc="c_v")
            last_gprod = C["c_v"]
            BL = CH // 4
            qoffs = [0, CH, CH + (CH - BL), CH + (CH - BL) + (CH - 2 * BL)]

            def emit_scoreprep(k):
                for j in range(4):
                    wj = CH - j * BL
                    qo = qoffs[j]
                    O("scalar", lambda e, j=j, wj=wj, qo=qo, k=k: e.activation(
                        qtb[:, qo:qo + wj],
                        cum[:, k * CH + j * BL:(k + 1) * CH], ACT.Exp,
                        scale=-1.0,
                        bias=cum[:, k * CH + j * BL + BL // 2:
                                 k * CH + j * BL + BL // 2 + 1]),
                      reads=("cum",), writes=("qtb",))
                    O("scalar", lambda e, j=j, k=k: e.activation(
                        kt0[:, j * BL:(j + 1) * BL],
                        cum[:, k * CH + j * BL:k * CH + (j + 1) * BL],
                        ACT.Exp, bias=nref[:, 4 * k + j:4 * k + j + 1]),
                      reads=("cum", "nref"), writes=("kt0",),
                      inc="c_s" if j == 3 else None)
                sc_exp = C["c_s"]
                Wt("vector", "c_s", sc_exp)
                for j in range(4):
                    wj = CH - j * BL
                    qo = qoffs[j]
                    O("vector", lambda e, j=j, wj=wj, qo=qo, k=k:
                      e.tensor_tensor(
                        qtb[:, qo:qo + wj],
                        rk1[:, k * CH + j * BL:(k + 1) * CH],
                        qtb[:, qo:qo + wj], ALU.mult),
                      reads=("rk1", "qtb"), writes=("qtb",))
                    O("vector", lambda e, j=j, k=k: e.tensor_tensor(
                        kt0[:, j * BL:(j + 1) * BL],
                        kq[:, k * CH + j * BL:k * CH + (j + 1) * BL],
                        kt0[:, j * BL:(j + 1) * BL], ALU.mult),
                      reads=("kq", "kt0"), writes=("kt0",))
                O("vector", lambda e: e.engine_nop(), inc="c_v")
                return C["c_v"]
            for k in range(CPM):
                O("vector", lambda e, k=k: e.tensor_tensor_scan(
                    cum[:, k * CH:(k + 1) * CH],
                    gprod[:, k * CH:(k + 1) * CH],
                    zeros64[:, :], 0.0, ALU.add, ALU.add),
                  reads=("gprod", "zeros64"), writes=("cum",),
                  inc="c_v" if k == CPM - 1 else None)
            cum_v = C["c_v"]
            Wt("scalar", "c_v", cum_v)
            O("scalar", lambda e: e.activation(
                e1t[:, :], cum[:, :], ACT.Exp, scale=-1.0,
                bias=lnscol[:, 0:1]),
              reads=("cum", "lnscol"), writes=("e1t",), inc="c_s")
            e1_s = C["c_s"]
            sn, sv = b03_free
            Wt("tensor", sn, sv)
            Wt("tensor", "d_g", b1_g)
            Wt("tensor", "c_s", last_projev)
            for nb in range(NB):
                O("tensor", lambda e, nb=nb: e.matmul(
                    PS[:, nb * 512:(nb + 1) * 512], onesbf[0:1, :],
                    b1row[0:1, nb * 512:(nb + 1) * 512],
                    start=True, stop=False))
            for nb in range(NB):
                O("tensor", lambda e, nb=nb: e.matmul(
                    PS[:, nb * 512:(nb + 1) * 512],
                    ch0sb[:, 0:CH], W1T[:, nb * 512:(nb + 1) * 512],
                    start=False, stop=True),
                  inc="c_t" if nb == NB - 1 else None)
            ch1_t = C["c_t"]
            Wt("scalar", "c_t", ch1_t)
            sn, sv = xbuf_free
            Wt("scalar", sn, sv)
            O("scalar", lambda e: e.activation(xbuf[:, :], PS[:, 0:D],
                                               ACT.Sigmoid),
              writes=("xbuf",), inc="c_s")
            sig_s = C["c_s"]
            b03_free = ("c_s", sig_s)
            Wt("vector", "c_s", e1_s)
            O("vector", lambda e: e.tensor_tensor(
                rt1[:, :].bitcast(F32R), rk1[:, :], e1t[:, :], ALU.mult),
              reads=("rk1", "e1t"), writes=("rt1",), inc="c_v")
            rt1_v = C["c_v"]
            O("vector", lambda e: e.tensor_tensor(
                itg[:, :], rk2[:, :], rk2[:, :], ALU.mult),
              reads=("rk2",), writes=("itg",), inc="c_v")
            Wt("tensor", "c_v", C["c_v"])
            Wt("tensor", "c_v", last_dsdec)
            O("tensor", lambda e: e.matmul(
                PS[0:1, 2560:2560 + MAC], onecol[0:M, 0:1], itg[:, :],
                start=True, stop=True), inc="c_t")
            kn_t = C["c_t"]
            Wt("vector", "c_t", kn_t)
            O("vector", lambda e: e.tensor_scalar(
                knrow[0:1, 0:MAC], PS[0:1, 2560:2560 + MAC], KNORM_EPS2,
                None, ALU.max), writes=("knrow",), inc="c_v")
            Wt("scalar", "c_v", C["c_v"])
            O("scalar", lambda e: e.activation(
                knrow[0:1, 0:MAC], knrow[0:1, 0:MAC], ACT.Ln),
              reads=("knrow",), writes=("knrow",))
            O("scalar", lambda e: e.activation(
                knrow[0:1, 0:MAC], knrow[0:1, 0:MAC], ACT.Exp, scale=-0.5),
              reads=("knrow",), writes=("knrow",), inc="c_s")
            kn_s = C["c_s"]
            Wt("tensor", "c_s", kn_s)
            O("tensor", lambda e: e.matmul(
                PS[0:M, 2560:2560 + MAC], ones1r[0:1, 0:M],
                knrow[0:1, 0:MAC], start=True, stop=True), inc="c_t")
            knb_t = C["c_t"]
            O("scalar", lambda e: e.activation(
                itg[:, :], selb[:, :], ACT.Ln),
              reads=("selb",), writes=("itg",), inc="c_s")
            Wt("vector", "c_s", C["c_s"])
            O("vector", lambda e: e.tensor_tensor(
                itg[:, :], itg[:, :], taub[:, :], ALU.mult),
              reads=("itg", "taub"), writes=("itg",), inc="c_v")
            Wt("scalar", "c_v", C["c_v"])
            O("scalar", lambda e: e.activation(itg[:, :], itg[:, :], ACT.Exp),
              reads=("itg",), writes=("itg",), inc="c_s")
            itg_s = C["c_s"]
            Wt("vector", "c_t", knb_t)
            Wt("vector", "c_s", itg_s)
            O("vector", lambda e: e.scalar_tensor_tensor(
                kq[:, :], rk2[:, :], 1.0, PS[0:M, 2560:2560 + MAC],
                ALU.bypass, ALU.mult),
              reads=("rk2",), writes=("kq",))
            O("vector", lambda e: e.tensor_tensor(
                kq[:, :], kq[:, :], itg[:, :], ALU.mult),
              reads=("kq", "itg"), writes=("kq",), inc="c_v")
            last_kq = C["c_v"]
            O("vector", lambda e: e.tensor_scalar(
                nref[:, 0:8],
                cum[:, :].rearrange("p (a f) -> p a f", f=32)[:, :, 16:17]
                .rearrange("p a one -> p (a one)"),
                -1.0, None, ALU.mult),
              reads=("cum",), writes=("nref",))
            O("vector", lambda e: e.tensor_scalar(
                nref[:, 8:10],
                cum[:, :].rearrange("p (a f) -> p a f", f=CH)[:, :, CH - 1:CH]
                .rearrange("p a one -> p (a one)"),
                -1.0, None, ALU.mult),
              reads=("cum",), writes=("nref",), inc="c_v")
            nref_v = C["c_v"]
            Wt("scalar", "c_v", nref_v)
            for k in range(CPM):
                O("scalar", lambda e, k=k: e.activation(
                    e2t[:, k * CH:(k + 1) * CH], cum[:, k * CH:(k + 1) * CH],
                    ACT.Exp, bias=nref[:, 8 + k:9 + k]),
                  reads=("cum", "nref"), writes=("e2t",))
            O("scalar", lambda e: e.activation(
                edec[:, 0:2], nref[:, 8:10], ACT.Exp),
              reads=("nref",), writes=("edec",), inc="c_s")
            e2_s = C["c_s"]
            Wt("vector", "c_s", e2_s)
            O("vector", lambda e: e.tensor_tensor(
                e2t[:, :], kq[:, :], e2t[:, :], ALU.mult),
              reads=("kq", "e2t"), writes=("e2t",), inc="c_v")
            khat_v = C["c_v"]
            sc_mul = emit_scoreprep(0)
            Wt("vector", "c_t", last_LN3_tr)
            O("vector", lambda e: e.tensor_scalar(
                xl2c[:, :], x1mac[:, 0:D], m2a2[:, 1:2], m2a2[:, 0:1],
                ALU.mult, ALU.add),
              reads=("x1mac", "m2a2"), writes=("xl2c",))
            Wt("vector", "c_s", sig_s)
            O("vector", lambda e: e.tensor_tensor(
                Ubuf[:, :], xbuf[:, :], xl2c[:, :], ALU.mult),
              reads=("xbuf", "xl2c"), writes=("Ubuf",), inc="c_v")
            last_Ut = C["c_v"]
            xbuf_free = ("c_v", last_Ut)
            for c in range(DCH):
                O("vector", lambda e, c=c: e.tensor_scalar(
                    accs[:, c, :].bitcast(F32R), accs[:, c, :],
                    rw_fm[:, c:c + 1], None, ALU.mult),
                  reads=("shb#%d" % c, "rw_fm"), writes=("shb#%d" % c,))
            O("vector", lambda e: e.engine_nop(), inc="c_v")
            last_sh2 = C["c_v"]

            for k in range(CPM):
                tg = b * NTB + mi * CPM + k
                ksl = slice(k * CH, (k + 1) * CH)
                if k > 0:
                    sc_mul = emit_scoreprep(k)
                Wt("tensor", "c_v", sc_mul)
                if tg > 0:
                    Wt("tensor", "c_v", last_PT_ev)
                for j in range(4):
                    wj = CH - j * BL
                    qo = qoffs[j]
                    O("tensor", lambda e, j=j, wj=wj, qo=qo: e.matmul(
                        PS[j * BL:(j + 1) * BL, 2816 + j * BL:2816 + CH],
                        kt0[:, j * BL:(j + 1) * BL], qtb[:, qo:qo + wj],
                        start=True, stop=True,
                        tile_position=(0, (j * BL) // 32 * 32)),
                      inc="c_t" if j == 3 else None)
                pt_t = C["c_t"]
                Wt("vector", "c_t", pt_t)
                O("vector", lambda e: e.tensor_tensor(
                    PTsb[:, :], PS[:, 2816:2816 + CH], maskt[:, :], ALU.mult),
                  reads=("maskt",), writes=("PTsb",), inc="c_v")
                last_PT_ev = C["c_v"]
                Wt("tensor", "c_v", max(khat_v, last_PT_ev, last_kq))
                O("tensor", lambda e, ksl=ksl: e.transpose(
                    PS[0:CH, 2560:2560 + M], e2t[:, ksl], ident32[0:M, 0:M]),
                  inc="c_t")
                khT_t = C["c_t"]
                Wt("vector", "c_t", khT_t)
                O("vector", lambda e: e.tensor_copy(
                    khatT[:, :], PS[0:CH, 2560:2560 + M]),
                  writes=("khatT",), inc="c_v")
                last_khatT_ev = C["c_v"]

                sn, sv = b03_free
                Wt("tensor", sn, sv)
                Wt("tensor", "c_v", max(last_Ut, last_PT_ev, rt1_v))
                first_chunk = (mi == 0 and k == 0)
                for nb in range(NB):
                    O("tensor", lambda e, nb=nb, fc=first_chunk: e.matmul(
                        PS[:, nb * 512:(nb + 1) * 512], PTsb[:, :],
                        Ubuf[:, nb * 512:(nb + 1) * 512],
                        start=True, stop=fc),
                      inc="c_t" if (first_chunk and nb == NB - 1) else None)
                if not first_chunk:
                    for nb in range(NB):
                        O("tensor", lambda e, nb=nb, ksl=ksl: e.matmul(
                            PS[:, nb * 512:(nb + 1) * 512],
                            rt1[:, ksl].bitcast(F32R),
                            Ssb[:, nb * 512:(nb + 1) * 512].bitcast(F32R),
                            start=False, stop=True),
                          inc="c_t" if nb == NB - 1 else None)
                last_o_t = C["c_t"]
                Wt("vector", "c_t", last_o_t)
                O("vector", lambda e, k=k: e.scalar_tensor_tensor(
                    x1mac[:, k * D:(k + 1) * D], PS[:, 0:D], 1.0,
                    x1mac[:, k * D:(k + 1) * D], ALU.bypass, ALU.add),
                  reads=("x1mac",), writes=("x1mac",), inc="c_v")
                x2a_v = C["c_v"]
                b03_free = ("c_v", x2a_v)
                Wt("tensor", "c_v", max(last_khatT_ev, last_kq))
                bf4 = bank_free[4]
                bf5 = bank_free[5]
                Wt("tensor", bf4[0], bf4[1])
                Wt("tensor", bf5[0], bf5[1])
                for nb in range(NB):
                    O("tensor", lambda e, nb=nb: e.matmul(
                        PS[0:M, 2048 + nb * 512:2048 + (nb + 1) * 512],
                        khatT[:, :], Ubuf[:, nb * 512:(nb + 1) * 512],
                        start=True, stop=True),
                      inc="c_t" if nb == NB - 1 else None)
                ds_t = C["c_t"]
                if k + 1 < CPM:
                    sn, sv = b03_free
                    Wt("tensor", sn, sv)
                    Wt("tensor", "d_g", b1_g)
                    for nb in range(NB):
                        O("tensor", lambda e, nb=nb: e.matmul(
                            PS[:, nb * 512:(nb + 1) * 512], onesbf[0:1, :],
                            b1row[0:1, nb * 512:(nb + 1) * 512],
                            start=True, stop=False))
                    for nb in range(NB):
                        O("tensor", lambda e, nb=nb, k=k: e.matmul(
                            PS[:, nb * 512:(nb + 1) * 512],
                            ch0sb[:, (k + 1) * CH:(k + 2) * CH],
                            W1T[:, nb * 512:(nb + 1) * 512],
                            start=False, stop=True),
                          inc="c_t" if nb == NB - 1 else None)
                    ch1_t = C["c_t"]
                    Wt("scalar", "c_t", ch1_t)
                    sn, sv = xbuf_free
                    Wt("scalar", sn, sv)
                    O("scalar", lambda e: e.activation(
                        xbuf[:, :], PS[:, 0:D], ACT.Sigmoid),
                      writes=("xbuf",), inc="c_s")
                    sig_s = C["c_s"]
                    b03_free = ("c_s", sig_s)
                Wt("vector", "c_t", ds_t)
                if first_chunk:
                    O("vector", lambda e: e.tensor_scalar(
                        Ssb[:, :].bitcast(F32R), PS[0:M, 2048:2048 + D], 1.0,
                        None, ALU.mult),
                      writes=("Ssb",), inc="c_v")
                else:
                    O("vector", lambda e, k=k: e.scalar_tensor_tensor(
                        Ssb[:, :].bitcast(F32R), Ssb[:, :], edec[:, k:k + 1],
                        PS[0:M, 2048:2048 + D], ALU.mult, ALU.add),
                      reads=("Ssb", "edec"), writes=("Ssb",), inc="c_v")
                last_dsdec = C["c_v"]
                bank_free[4] = ("c_v", last_dsdec)
                bank_free[5] = ("c_v", last_dsdec)
                if k + 1 < CPM:
                    O("vector", lambda e, k=k: e.tensor_scalar(
                        xl2c[:, :], x1mac[:, (k + 1) * D:(k + 2) * D],
                        m2a2[:, 2 * k + 3:2 * k + 4],
                        m2a2[:, 2 * k + 2:2 * k + 3], ALU.mult, ALU.add),
                      reads=("x1mac", "m2a2"), writes=("xl2c",))
                    Wt("vector", "c_s", sig_s)
                    O("vector", lambda e: e.tensor_tensor(
                        Ubuf[:, :], xbuf[:, :], xl2c[:, :], ALU.mult),
                      reads=("xbuf", "xl2c"), writes=("Ubuf",), inc="c_v")
                    last_Ut = C["c_v"]
                    xbuf_free = ("c_v", last_Ut)

                Wt("tensor", "c_v", max(last_sh2, last_dsdec))
                for g in range(NG):
                    bk = 4 + (g % 2)
                    sname, sval = bank_free[bk]
                    Wt("tensor", sname, sval)
                    nch = min(4, DCH - g * 4)
                    for j in range(nch):
                        c = g * 4 + j
                        O("tensor", lambda e, c=c, bk=bk, j=j, k=k: e.transpose(
                            PS[:, bank_cols(bk) + j * 128:
                               bank_cols(bk) + (j + 1) * 128],
                            shr[:, c, k * CH:(k + 1) * CH], ident32[:, :]),
                          inc="c_t")
                    sh_tr = C["c_t"]
                    Wt("vector", "c_t", sh_tr)
                    O("vector", lambda e, g=g, k=k, bk=bk, nch=nch:
                      e.scalar_tensor_tensor(
                        x1mac[:, k * D + g * 512:k * D + g * 512 + nch * 128],
                        PS[:, bank_cols(bk):bank_cols(bk) + nch * 128], 1.0,
                        x1mac[:, k * D + g * 512:k * D + g * 512 + nch * 128],
                        ALU.bypass, ALU.add),
                      reads=("x1mac",), writes=("x1mac",), inc="c_v")
                    last_x2b = C["c_v"]
                    bank_free[bk] = ("c_v", last_x2b)

                if k == CPM - 1 and tg + 1 < NT:
                    tgn = tg + 1
                    kkn = tgn % 2
                    sn, sv = xbuf_free
                    Wt("sync", sn, sv)
                    O("sync", lambda e, tgn=tgn: e.dma_start(
                        out=xbuf[:, :],
                        in_=x_in[tgn * 128:(tgn + 1) * 128, :]),
                      inc="d_ldx")
                    hoist["ldx"] = C["d_ldx"]
                    Wt("tensor", "c_v", qstate["last_xq_any"])
                    sn, sv = b03_free
                    Wt("tensor", sn, sv)
                    hbk = [0] * NB
                    for nb in range(2):
                        for c in range(DCH):
                            O("tensor", lambda e, c=c, nb=nb, kkn=kkn:
                              e.matmul(
                                PS[:, nb * 512:(nb + 1) * 512],
                                xqT[:, kkn * D + c * 128:
                                    kkn * D + (c + 1) * 128],
                                WqT[:, c * D + nb * 512:
                                    c * D + (nb + 1) * 512],
                                start=(c == 0), stop=(c == DCH - 1)),
                              inc="c_t" if c == DCH - 1 else None)
                        hbk[nb] = C["c_t"]
                ngrp = max(1, D // 512)
                for g in range(ngrp):
                    O("vector", lambda e, g=g, k=k: e.bn_stats(
                        mvb[:, 6 * g:6 * (g + 1)],
                        x1mac[:, k * D + 512 * g:k * D + 512 * (g + 1)]),
                      reads=("x1mac", "mvb"), writes=("mvb",))
                O("vector", lambda e, ngrp=ngrp: e.bn_aggr(
                    mv[:, 0:2], mvb[:, 0:6 * ngrp]),
                  reads=("mvb",), writes=("mv",), inc="c_v")
                rsqrt_chain(mv[:, 1:2], acol[:, 0:1], 1.0 + RMS_EPS,
                            RMS_EPS * LN_EPS, "mv", "acol")
                Wt("vector", "c_s", C["c_s"])
                O("vector", lambda e: e.tensor_scalar(
                    nmacol[:, 0:1], mv[:, 0:1], acol[:, 0:1], -1.0, ALU.mult,
                    ALU.mult), reads=("mv", "acol"), writes=("nmacol",),
                  inc="c_v")
                O("vector", lambda e, k=k: e.tensor_scalar(
                    xl2c[:, :], x1mac[:, k * D:(k + 1) * D], acol[:, 0:1],
                    nmacol[:, 0:1], ALU.mult, ALU.add),
                  reads=("x1mac", "nmacol", "acol"), writes=("xl2c",),
                  inc="c_v")
                v3_v = C["c_v"]
                last_v3 = v3_v
                Wt("scalar", "c_v", v3_v)
                O("scalar", lambda e, tg=tg, k=k: e.dma_start(
                    out=x2_dram.ap()[tg * 128:(tg + 1) * 128, :],
                    in_=x1mac[:, k * D:(k + 1) * D]), inc="d_stx2")
                last_x2_spill = C["d_stx2"]
                Wt("tensor", "c_v", v3_v)
                for g in range(NG):
                    bk = 4 + (g % 2)
                    sname, sval = bank_free[bk]
                    Wt("tensor", sname, sval)
                    nch = min(4, DCH - g * 4)
                    for j in range(nch):
                        c = g * 4 + j
                        O("tensor", lambda e, c=c, bk=bk, j=j: e.transpose(
                            PS[:, bank_cols(bk) + j * 128:
                               bank_cols(bk) + (j + 1) * 128],
                            xl2c[:, c * 128:(c + 1) * 128], ident32[:, :]),
                          inc="c_t")
                    tr3 = C["c_t"]
                    last_LN3_tr = tr3
                    qstate["xl2c_free_t"] = tr3
                    Wt("scalar", "c_t", tr3)
                    if g == 0:
                        Wt("scalar", "d_stu", last_u3_spill)
                        Wt("scalar", "c_v", last_u3_red)
                    O("scalar", lambda e, g=g, bk=bk, nch=nch: e.copy(
                        u3buf[:, g * 512:g * 512 + nch * 128],
                        PS[:, bank_cols(bk):bank_cols(bk) + nch * 128]),
                      writes=("u3buf#%d" % g,), inc="c_s")
                    ev3 = C["c_s"]
                    bank_free[bk] = ("c_s", ev3)
                    Wt("vector", "c_s", ev3)
                    O("vector", lambda e, g=g, nch=nch: e.tensor_reduce(
                        tmp4[:, (g % 2) * 4:(g % 2) * 4 + nch],
                        u3buf[:, g * 512:g * 512 + nch * 128].rearrange(
                            "p (c f) -> p c f", f=128),
                        AX.X, ALU.max, apply_absolute_value=True),
                      reads=("u3buf#%d" % g,), writes=("tmp4#%d" % (g % 2),))
                    O("vector", lambda e, g=g, nch=nch: e.tensor_tensor(
                        cm3run[:, g * 4:g * 4 + nch],
                        cm3run[:, g * 4:g * 4 + nch],
                        tmp4[:, (g % 2) * 4:(g % 2) * 4 + nch], ALU.max),
                      reads=("tmp4#%d" % (g % 2), "cm3run#%d" % g),
                      writes=("cm3run#%d" % g,), inc="c_v")
                    last_u3_red = C["c_v"]
                    Wt("scalar", "c_s", ev3)
                    O("scalar", lambda e, tg=tg, g=g, nch=nch: e.dma_start(
                        out=u3_dram.ap()[tg:tg + 1,
                                         g * 4 * 128 * 128:
                                         (g * 4 + nch) * 128 * 128]
                        .rearrange("one (c p f) -> (one p) c f", p=128, f=128),
                        in_=u3buf[:, g * 512:g * 512 + nch * 128]),
                      inc="d_stu")
                    last_u3_spill = C["d_stu"]
                if k == CPM - 1 and tg + 1 < NT:
                    for nb in range(2, NB):
                        for c in range(DCH):
                            O("tensor", lambda e, c=c, nb=nb, kkn=kkn:
                              e.matmul(
                                PS[:, nb * 512:(nb + 1) * 512],
                                xqT[:, kkn * D + c * 128:
                                    kkn * D + (c + 1) * 128],
                                WqT[:, c * D + nb * 512:
                                    c * D + (nb + 1) * 512],
                                start=(c == 0), stop=(c == DCH - 1)),
                              inc="c_t" if c == DCH - 1 else None)
                        hbk[nb] = C["c_t"]
                    hoist["tg"] = tgn
                    hoist["ffn_bk"] = hbk
                qstate["u3s"] = C["c_s"]
                qstate["u3d"] = last_u3_spill

    ph3_ready = allreduce_scale("cm3run", cm3run, cc2_in, cc2_out, 8,
                                c1bcol, cebcol, gamc1b, "gamc1b")

    Wt("sync", "c_v", ph3_ready)
    Wt("sync", "d_stx2", C["d_stx2"])
    Wt("sync", "d_stu", C["d_stu"])
    ph3_ffn = [last_ffn[0], last_ffn[1]]
    ph3_y = 0
    ph3_ystore = 0
    q3 = {"last_xq": qstate["last_xq_any"]}

    def emit_quant3_round(t):
        kk = t % 2
        O("vector", lambda e: e.tensor_scalar(
            qbuf[:, :], qbuf[:, :], 12582912.0, None, ALU.add),
          reads=("qbuf",), writes=("qbuf",))
        if t > 1:
            Wt("vector", "c_t", ph3_ffn[kk])
        O("vector", lambda e, kk=kk: e.tensor_scalar(
            xqT[:, kk * D:(kk + 1) * D], qbuf[:, :], -12582912.0, None,
            ALU.add),
          reads=("qbuf",), writes=("xqT",), inc="c_v")
        q3["last_xq"] = C["c_v"]

    def emit_quant3(t):
        Wt("sync", "c_v", q3["last_xq"])
        ldu = [0] * 4
        for gq in range(4):
            O("sync", lambda e, t=t, gq=gq: e.dma_start(
                out=qbuf[:, gq * 512:(gq + 1) * 512],
                in_=u3_dram.ap()[t:t + 1,
                                 gq * 4 * 128 * 128:(gq + 1) * 4 * 128 * 128]
                .rearrange("one (c p f) -> (one p) c f", p=128, f=128)),
              inc="d_ldu")
            ldu[gq] = C["d_ldu"]
            Wt("sync", "d_ldu", ldu[gq])
        if t == 0:
            Wt("vector", "c_t", last_LN3_tr)
        for gq in range(4):
            emit_quant_scale(gq, ldu, gamc1b, "gamc1b")
        emit_quant3_round(t)

    emit_quant3(0)
    y_bk = [0] * NB
    ysto_bk = [0] * NB
    for t in range(NT):
        kk = t % 2
        if t > 0:
            Wt("sync", "c_v", ph3_y)
        else:
            sn, sv = xbuf_free
            Wt("sync", sn, sv)
        O("sync", lambda e, t=t: e.dma_start(
            out=xbuf[:, :], in_=x2_dram.ap()[t * 128:(t + 1) * 128, :]),
          inc="d_ldx")
        lx = C["d_ldx"]
        Wt("tensor", "c_v", q3["last_xq"])
        ffn3_bk = [0] * NB
        for nb in range(NB):
            if t == 0:
                if nb == 0:
                    sn, sv = b03_free
                    Wt("tensor", sn, sv)
            else:
                Wt("tensor", "c_v", y_bk[nb])
            for c in range(DCH):
                O("tensor", lambda e, c=c, nb=nb, kk=kk: e.matmul(
                    PS[:, nb * 512:(nb + 1) * 512],
                    xqT[:, kk * D + c * 128:kk * D + (c + 1) * 128],
                    WqT[:, c * D + nb * 512:c * D + (nb + 1) * 512],
                    start=(c == 0), stop=(c == DCH - 1)),
                  inc="c_t" if c == DCH - 1 else None)
            ffn3_bk[nb] = C["c_t"]
        ph3_ffn[kk] = C["c_t"]
        do_q = t + 1 < NT
        if do_q:
            Wt("sync", "c_v", q3["last_xq"])
            qldu = [0] * 4
            for gq in range(4):
                O("sync", lambda e, t=t, gq=gq: e.dma_start(
                    out=qbuf[:, gq * 512:(gq + 1) * 512],
                    in_=u3_dram.ap()[t + 1:t + 2,
                                     gq * 4 * 128 * 128:
                                     (gq + 1) * 4 * 128 * 128]
                    .rearrange("one (c p f) -> (one p) c f", p=128, f=128)),
                  inc="d_ldu")
                qldu[gq] = C["d_ldu"]
                Wt("sync", "d_ldu", qldu[gq])
        Wt("vector", "d_ldx", lx)
        for nb in range(NB):
            if do_q:
                emit_quant_scale(nb, qldu, gamc1b, "gamc1b")
                if nb == NB - 1:
                    O("vector", lambda e: e.tensor_scalar(
                        qbuf[:, :], qbuf[:, :], 12582912.0, None, ALU.add),
                      reads=("qbuf",), writes=("qbuf",))
            Wt("vector", "c_t", ffn3_bk[nb])
            if t > 0:
                Wt("vector", "d_sty", ysto_bk[nb])
            O("vector", lambda e, nb=nb: e.scalar_tensor_tensor(
                u3buf[:, nb * 512:(nb + 1) * 512],
                PS[:, nb * 512:(nb + 1) * 512], cebcol[:, 0:1],
                xbuf[:, nb * 512:(nb + 1) * 512], ALU.mult, ALU.add),
              reads=("xbuf", "cebcol"), writes=("u3buf#%d" % nb,),
              inc="c_v")
            y_bk[nb] = C["c_v"]
            Wt("scalar", "c_v", y_bk[nb])
            O("scalar", lambda e, t=t, nb=nb: e.dma_start(
                out=y_out[t * 128:(t + 1) * 128, nb * 512:(nb + 1) * 512],
                in_=u3buf[:, nb * 512:(nb + 1) * 512]), inc="d_sty")
            ysto_bk[nb] = C["d_sty"]
            Wt("scalar", "d_sty", ysto_bk[nb])
        ph3_y = y_bk[NB - 1]
        if do_q:
            kk2 = (t + 1) % 2
            if t + 1 > 1:
                Wt("vector", "c_t", ph3_ffn[kk2])
            O("vector", lambda e, kk2=kk2: e.tensor_scalar(
                xqT[:, kk2 * D:(kk2 + 1) * D], qbuf[:, :], -12582912.0, None,
                ALU.add),
              reads=("qbuf",), writes=("xqT",), inc="c_v")
            q3["last_xq"] = C["c_v"]

    Wt("sync", "d_sty", C["d_sty"])
    p.emit_all()
    p.ctx.close()
    return nc


def make_in_maps(cfg, inputs, n_cores):
    D, L, B = cfg.D, cfg.L, cfg.B
    x = np.ascontiguousarray(inputs["L0_attn_hidden_states"], np.float32)
    conv_w = np.asarray(inputs["L0_attn_conv_w"], np.float32)
    conv_b = np.asarray(inputs["L0_attn_conv_b"], np.float32)
    w_fm = np.ascontiguousarray(
        conv_w.reshape(cfg.DCH, 128, 4).transpose(1, 0, 2).reshape(128, -1))
    b_fm = np.ascontiguousarray(conv_b.reshape(cfg.DCH, 128).T)
    rw_fm = np.ascontiguousarray(
        np.asarray(inputs["L0_attn_residual_weight"], np.float32)
        .reshape(cfg.DCH, 128).T)
    gamma2d = np.ascontiguousarray(
        np.asarray(inputs["L0_ffn_norm_weight"], np.float32)
        .reshape(cfg.DCH, 128))
    chw1 = np.asarray(inputs["L0_attn_ch_gate_w1"], np.float32)
    chw1r = np.ascontiguousarray(
        chw1.reshape(cfg.DCH, 128, cfg.RK).transpose(1, 0, 2).reshape(128, -1))
    memb = np.asarray(inputs["L0_attn_mem_gate_b"], np.float32)
    memb2 = np.ascontiguousarray(memb.reshape(2, cfg.M).T)
    import ml_dtypes
    ident32 = np.eye(128, dtype=np.float32)
    sidx = np.arange(128)[:, None]
    tidx = np.arange(128)[None, :]
    mask = np.where(tidx >= sidx, np.float32(cfg.SCALE),
                    np.float32(0.0)).astype(np.float32)
    common = {
        "ffn_w": np.ascontiguousarray(inputs["L0_ffn_weight"], np.float32),
        "gamma2d": gamma2d, "conv_wfm": w_fm, "conv_bfm": b_fm,
        "rw_fm": rw_fm,
        "inproj": np.ascontiguousarray(inputs["L0_attn_in_proj_w"], np.float32),
        "chw0": np.ascontiguousarray(inputs["L0_attn_ch_gate_w0"], np.float32),
        "chw1r": chw1r,
        "chb1row": np.ascontiguousarray(
            np.asarray(inputs["L0_attn_ch_gate_b1"], np.float32).reshape(1, D)),
        "memw": np.ascontiguousarray(inputs["L0_attn_mem_gate_w"], np.float32),
        "memb2": memb2,
        "ident32": ident32, "mask": mask,
    }
    in_maps = []
    for c in range(n_cores):
        m = dict(common)
        m["x"] = np.ascontiguousarray(x[c * B:(c + 1) * B].reshape(B * L, D))
        in_maps.append(m)
    return in_maps


def kernel(**inputs):
    cfg = CFG()
    in_maps = make_in_maps(cfg, inputs, cfg.n_cores)
    nc = build(cfg)
    res = run_bass_kernel_spmd(nc, in_maps, core_ids=list(range(cfg.n_cores)))
    out = np.concatenate(
        [r["y"].reshape(cfg.B, cfg.L, cfg.D) for r in res.results], axis=0)
    return out.astype(np.float32)
